# version 15
# baseline (speedup 1.0000x reference)
import numpy as np
import ml_dtypes
from contextlib import ExitStack

import concourse.bass as bass
import concourse.mybir as mybir
from concourse.bass_utils import run_bass_kernel_spmd

F32 = mybir.dt.float32
BF16 = mybir.dt.bfloat16
I8 = mybir.dt.int8
OP = mybir.AluOpType
AF = mybir.ActivationFunctionType
AX = mybir.AxisListType.X

T, D, H, DK = 2048, 1024, 8, 128
NCH, CH = 16, 128
DH, NHC = 4096, 32
TB, NTB = 256, 8
LNEPS = 1e-5

ENGS = ("sy", "gp", "pe", "dv", "ac")


class Sched:

    def __init__(self):
        self.pending = {e: [] for e in ENGS}
        self.cnt = {e: 0 for e in ENGS}

    def add(self, eng, emit, waits=(), n_dma=0):
        amt = 16 * n_dma if n_dma else 1
        self.cnt[eng] += amt
        tok = (eng, self.cnt[eng])
        ws = {}
        for w in waits:
            if w is None:
                continue
            we, wv = w
            ws[we] = max(ws.get(we, 0), wv)
        self.pending[eng].append((ws, emit, n_dma))
        return tok

    def flush(self, block, sems):
        bodies = {"sy": block.sync, "gp": block.gpsimd, "pe": block.tensor,
                  "dv": block.vector, "ac": block.scalar}
        for eng_name, groups in self.pending.items():
            if not groups:
                continue
            def body(eng, groups=groups, eng_name=eng_name):
                for ws, emit, n_dma in groups:
                    for we, wv in sorted(ws.items()):
                        eng.wait_ge(sems[we], wv)
                    insts = emit(eng)
                    if not isinstance(insts, (list, tuple)):
                        insts = [insts]
                    if n_dma:
                        assert len(insts) == n_dma
                        for i in insts:
                            i.then_inc(sems[eng_name], 16)
                    else:
                        insts[-1].then_inc(sems[eng_name], 1)
            bodies[eng_name](body)
        self.pending = {e: [] for e in ENGS}


def _hview(ap):
    return ap.rearrange("p (h f) -> p h f", h=8)


def _hview16(ap):
    return ap.rearrange("p (h f) -> p h f", h=16)


def build_program(dbg=False):
    nc = bass.Bass("TRN2", target_bir_lowering=False)

    dx = nc.dram_tensor("x", [T, D], F32, kind="ExternalInput").ap()
    dxt = nc.dram_tensor("xt", [D, T], F32, kind="ExternalInput").ap()
    dwq = nc.dram_tensor("wq", [D, D], F32, kind="ExternalInput").ap()
    dwk = nc.dram_tensor("wk", [D, D], F32, kind="ExternalInput").ap()
    dwv = nc.dram_tensor("wv", [D, D], F32, kind="ExternalInput").ap()
    dwg = nc.dram_tensor("wg", [D, D], F32, kind="ExternalInput").ap()
    dwo = nc.dram_tensor("wo", [D, D], F32, kind="ExternalInput").ap()
    dw1t = nc.dram_tensor("w1t", [D, DH], F32, kind="ExternalInput").ap()
    dw2t = nc.dram_tensor("w2t", [DH, D], F32, kind="ExternalInput").ap()
    dcw = nc.dram_tensor("cw", [128, 3 * NHC], F32, kind="ExternalInput").ap()
    dcb = nc.dram_tensor("cb", [128, NHC], F32, kind="ExternalInput").ap()
    didn = nc.dram_tensor("idn", [128, 128], BF16, kind="ExternalInput").ap()
    dmac = nc.dram_tensor("mac", [128, 130], BF16, kind="ExternalInput").ap()
    dmb = nc.dram_tensor("mbm", [128, 128], BF16, kind="ExternalInput").ap()
    dmask = nc.dram_tensor("mask", [128, 128], BF16, kind="ExternalInput").ap()
    dmasku = nc.dram_tensor("masku", [128, 128], mybir.dt.uint8, kind="ExternalInput").ap()
    dones1b = nc.dram_tensor("ones1b", [1, 128], BF16, kind="ExternalInput").ap()
    donescf = nc.dram_tensor("onescf", [128, 1], F32, kind="ExternalInput").ap()
    donesrf = nc.dram_tensor("onesrf", [1, 128], F32, kind="ExternalInput").ap()

    scratch_kind = "ExternalOutput" if dbg else "Internal"
    ddbg = {}
    if dbg:
        for nm, shp, dt in (("dqn", [128, D], BF16), ("dkn", [128, D], BF16),
                            ("dvt", [128, D], BF16), ("dgp", [128, D], BF16),
                            ("dE1a", [128, D], F32), ("dE1b", [128, 16], F32),
                            ("dE2", [128, D], F32), ("dE3", [128, D], BF16),
                            ("dqf", [128, D], BF16), ("dkf", [128, D], BF16),
                            ("dPT", [128, D], BF16), ("dOT", [128, D], BF16),
                            ("dxp", [128, D], F32), ("dS", [128, D], BF16),
                            ("dxb", [128, D], F32), ("dlnst", [128, 10], F32),
                            ("dq8", [128, D], F32), ("dqss", [128, 16], F32),
                            ("dH2", [128, NHC * TB], BF16), ("dmt", [128, 8], F32),
                            ("dy2", [128, D], F32), ("dMWC", [128, 2], F32),
                            ("dXQT", [128, 8 * (T + 32)], BF16),
                            ("dsrb1", [128, 256], BF16), ("dmt1", [128, 8], F32),
                            ("dbp1", [128, D], F32)):
            ddbg[nm] = nc.dram_tensor(nm, shp, dt, kind="ExternalOutput").ap()
    dx1s = nc.dram_tensor("x1s", [T, D], F32, kind=scratch_kind).ap()
    dx1q = nc.dram_tensor("x1qd", [T, D], BF16, kind=scratch_kind).ap()
    dw1q = nc.dram_tensor("w1qd", [D, DH], I8, kind=scratch_kind).ap()
    dw2q = nc.dram_tensor("w2qd", [DH, D], I8, kind=scratch_kind).ap()
    dout = nc.dram_tensor("out", [T, D], F32, kind="ExternalOutput").ap()

    sc = Sched()
    tk = {}

    with ExitStack() as top:
        e = top.enter_context
        sems = {n: e(nc.semaphore("sem_" + n)) for n in ENGS}
        block = e(nc.Block())
        C_idn = e(nc.sbuf_tensor("C_idn", [128, 128], BF16))
        C_mac = e(nc.sbuf_tensor("C_mac", [128, 130], BF16))
        C_mb = e(nc.sbuf_tensor("C_mb", [128, 128], BF16))
        C_mask = e(nc.sbuf_tensor("C_mask", [128, 128], BF16))
        C_masku = e(nc.sbuf_tensor("C_masku", [128, 128], mybir.dt.uint8))
        C_o1b = e(nc.sbuf_tensor("C_o1b", [1, 128], BF16))
        C_ocf = e(nc.sbuf_tensor("C_ocf", [128, 1], F32))
        C_orf = e(nc.sbuf_tensor("C_orf", [1, 128], F32))
        C_cw = e(nc.sbuf_tensor("C_cw", [128, 3 * NHC], F32))
        C_cb = e(nc.sbuf_tensor("C_cb", [128, NHC], F32))
        waccl = e(nc.sbuf_tensor("waccl", [128, 32], F32))
        wacc2 = e(nc.sbuf_tensor("wacc2", [128, 32], F32))
        wab = e(nc.sbuf_tensor("wab", [128, 2], F32))
        swrow = e(nc.sbuf_tensor("swrow", [1, 2], F32))
        MWC = e(nc.sbuf_tensor("MWC", [128, 2], F32))
        SRW = e(nc.sbuf_tensor("SRW", [128, 2], F32))
        wsc1 = e(nc.sbuf_tensor("wsc1", [128, 2], F32))
        C_e24 = e(nc.sbuf_tensor("C_e24", [128, 1], F32))

        t_const = sc.add("sy", lambda s: [
            s.dma_start(C_idn[:], didn), s.dma_start(C_mac[:], dmac),
            s.dma_start(C_mb[:], dmb), s.dma_start(C_mask[:], dmask),
            s.dma_start(C_o1b[:], dones1b), s.dma_start(C_ocf[:], donescf),
            s.dma_start(C_orf[:], donesrf), s.dma_start(C_cw[:], dcw),
            s.dma_start(C_cb[:], dcb),
            s.dma_start(C_masku[:], dmasku),
        ], n_dma=10)

        with ExitStack() as pa:
            ea = pa.enter_context
            PSf = ea(nc.psum_tensor("PSfA", [128, 3072], F32))
            PSb = ea(nc.psum_tensor("PSbA", [128, 2048], BF16))
            WQ = ea(nc.sbuf_tensor("WQ", [128, 8192], BF16))
            WK = ea(nc.sbuf_tensor("WK", [128, 8192], BF16))
            WV = ea(nc.sbuf_tensor("WV", [128, 8192], BF16))
            WG = ea(nc.sbuf_tensor("WG", [128, 8192], BF16))
            WO = ea(nc.sbuf_tensor("WO", [128, 8192], BF16))
            XT = ea(nc.sbuf_tensor("XT", [128, 16384], BF16))
            xb = [ea(nc.sbuf_tensor(f"xb{i}", [128, D], F32)) for i in range(2)]
            knqn = ea(nc.sbuf_tensor("knqn", [128, 2 * D], BF16))
            kn = knqn[:, 0:D]
            qn = knqn[:, D:2 * D]
            vtm = ea(nc.sbuf_tensor("vtm", [128, D], BF16))
            khat = ea(nc.sbuf_tensor("khat", [128, D], BF16))
            gp_ = ea(nc.sbuf_tensor("gp", [128, D], BF16))
            E1a = ea(nc.sbuf_tensor("E1a", [128, D], F32))
            E1b = ea(nc.sbuf_tensor("E1b", [128, 16], F32))
            E3 = ea(nc.sbuf_tensor("E3", [128, D], BF16))
            qf = ea(nc.sbuf_tensor("qf", [128, D], BF16))
            kf = ea(nc.sbuf_tensor("kf", [128, D], BF16))
            PT = ea(nc.sbuf_tensor("PT", [128, D], BF16))
            OT = ea(nc.sbuf_tensor("OT", [128, D], BF16))
            S = ea(nc.sbuf_tensor("S", [128, D], BF16))
            St = ea(nc.sbuf_tensor("St", [128, D], BF16))
            Sm = ea(nc.sbuf_tensor("Sm", [128, D], BF16))
            scrA = ea(nc.sbuf_tensor("scrA", [128, D], F32))
            scrB = ea(nc.sbuf_tensor("scrB", [128, D], F32))
            qss = ea(nc.sbuf_tensor("qss", [128, 16], F32))
            rqk = ea(nc.sbuf_tensor("rqk", [128, 16], F32))
            x1pre = [ea(nc.sbuf_tensor(f"x1pre{i}", [128, D], F32)) for i in range(2)]
            x1b = [ea(nc.sbuf_tensor(f"x1b{i}", [128, D], F32)) for i in range(2)]
            lnst = ea(nc.sbuf_tensor("lnst", [128, 10], F32))
            xq8 = ea(nc.sbuf_tensor("xq8", [128, D], I8))
            xqs = [ea(nc.sbuf_tensor(f"xqs{i}", [128, D], BF16)) for i in range(2)]
            wstage = [ea(nc.sbuf_tensor(f"wstage{i}", [128, 1024], F32)) for i in range(4)]
            wscr = ea(nc.sbuf_tensor("wscr", [128, 1024], F32))
            wq8 = [ea(nc.sbuf_tensor(f"wq8{i}", [128, 1024], I8)) for i in range(4)]

            def wload(g):
                ins = []
                for dst, src in ((WQ, dwq), (WK, dwk), (WV, dwv), (WG, dwg), (WO, dwo)):
                    ins.append(g.dma_start(dst[:], src.rearrange("(a p) f -> p a f", p=128)))
                ins.append(g.dma_start(XT[:], dxt.rearrange("(a p) t -> p a t", p=128)))
                return ins
            t_wload = sc.add("gp", wload, n_dma=6)
            t_smem = sc.add("dv", lambda v: [v.memset(S[:], 0.0),
                                             v.memset(PT[:], 0.0),
                                             v.memset(C_e24[:], 1e-24)])

            mean_pieces = [("w1", a, b) for a in range(8) for b in range(4)] + \
                          [("w2", g, 0) for g in range(32)]
            quant_pieces = list(mean_pieces)
            t_mload, t_mred, t_qload, t_qq, t_qst = {}, {}, {}, {}, {}
            wrec_tok = [None]
            wmw_tok = [None]

            def mean_piece_groups(i):
                which, a, b = mean_pieces[i]
                buf = wstage[i % 4]
                if which == "w1":
                    src = dw1t[128 * a:128 * a + 128, 1024 * b:1024 * b + 1024]
                else:
                    src = dw2t[128 * a:128 * a + 128, 0:D]
                t_mload[i] = sc.add("sy", lambda s, buf=buf, src=src:
                                    s.dma_start(buf[:], src),
                                    waits=[t_mred.get(i - 4)], n_dma=1)
                if which == "w1":
                    t_mred[i] = sc.add("dv", lambda v, buf=buf, i=i:
                                       v.tensor_reduce(waccl[:, i:i + 1], buf[:], AX,
                                                       OP.add, apply_absolute_value=True),
                                       waits=[t_mload[i]])
                else:
                    j = i - 32
                    t_mred[i] = sc.add("ac", lambda s, buf=buf, j=j:
                                       s.activation(wscr[:], buf[:], AF.Abs,
                                                    accum_out=wacc2[:, j:j + 1]),
                                       waits=[t_mload[i]])

            def quant_piece_groups(i):
                which, a, b = quant_pieces[i]
                buf = wstage[i % 4]
                q8 = wq8[i % 4]
                if which == "w1":
                    src = dw1t[128 * a:128 * a + 128, 1024 * b:1024 * b + 1024]
                    dst = dw1q[128 * a:128 * a + 128, 1024 * b:1024 * b + 1024]
                    scol = SRW[:, 0:1]
                else:
                    src = dw2t[128 * a:128 * a + 128, 0:D]
                    dst = dw2q[128 * a:128 * a + 128, 0:D]
                    scol = SRW[:, 1:2]
                t_qload[i] = sc.add("sy", lambda s, buf=buf, src=src:
                                    s.dma_start(buf[:], src),
                                    waits=[t_qq.get(i - 4), wrec_tok[0]], n_dma=1)
                t_qq[i] = sc.add("dv", lambda v, buf=buf, q8=q8, scol=scol: [
                    v.tensor_scalar(wscr[:], buf[:], scol, 1.0, OP.mult, OP.min),
                    v.tensor_scalar(q8[:], wscr[:], -1.0, None, OP.max),
                ], waits=[t_qload[i], t_qst.get(i - 4)])
                t_qst[i] = sc.add("gp", lambda g, dst=dst, q8=q8:
                                  g.dma_start(dst, q8[:]),
                                  waits=[t_qq[i]], n_dma=1)

            def fb(bank, off, n):
                return PSf[:, 512 * bank + off: 512 * bank + off + n]

            qsc = ea(nc.sbuf_tensor("qsc", [128, 6], F32))

            def em_headA(c):
                cm = c % 2
                tk[("xin", c)] = sc.add("sy", lambda s: s.dma_start(
                    xb[cm][:], dx[CH * c:CH * c + CH, :]),
                    waits=[tk.get(("resid", c - 2))], n_dma=1)

                def mkwave(W, base):
                    def wave(t):
                        ins = []
                        for dk in range(8):
                            xts = XT[:, 2048 * dk + CH * c: 2048 * dk + CH * c + CH]
                            for hf in range(2):
                                ins.append(t.matmul(
                                    PSf[:, base + 512 * hf: base + 512 * hf + 512], xts,
                                    W[:, 1024 * dk + 512 * hf: 1024 * dk + 512 * hf + 512],
                                    start=(dk == 0), stop=(dk == 7)))
                        return ins
                    return wave
                Wv = [tk.get(("ptm", c - 1, 7)), t_wload, t_const]
                if c == 8:
                    Wv.append(wmw_tok[0])
                if ("dbg0",) in tk and c == 1:
                    Wv.append(tk[("dbg0",)])
                tk[("wavev", c)] = sc.add("pe", mkwave(WV, 2048), waits=Wv)
                tk[("vevac", c)] = sc.add("ac", lambda s: s.copy(
                    vtm[:], PSf[:, 2048:3072]), waits=[tk[("wavev", c)]])
                tk[("wavez", c)] = sc.add("pe", mkwave(WG, 0),
                                          waits=[tk.get(("supd", c - 1))])
                tk[("gp", c)] = sc.add("ac", lambda s: [
                    s.activation(scrB[:], PSf[:, 0:1024], AF.Exp, scale=-1.0),
                    s.activation(gp_[:], scrB[:], AF.Ln, bias=1.0),
                ], waits=[tk[("wavez", c)], tk.get(("qfkf0", c - 1)),
                          tk.get(("qfkf", c - 1)), tk.get(("khat", c - 1))])
                tk[("waveq", c)] = sc.add("pe", mkwave(WQ, 1024),
                                          waits=[tk.get(("resid", c - 1))])

                def qss_ac(s):
                    return [s.activation(
                        scrB[:, 128 * h:128 * h + 128],
                        PSf[:, 1024 + 128 * h:1024 + 128 * h + 128],
                        AF.Square, accum_out=qss[:, 8 + h:9 + h]) for h in range(8)]
                tk[("qssg", c)] = sc.add("ac", qss_ac, waits=[tk[("waveq", c)]])
                def mkwave_half(W, base, hf):
                    def wave(t):
                        return [t.matmul(
                            PSf[:, base + 512 * hf: base + 512 * hf + 512],
                            XT[:, 2048 * dk + CH * c: 2048 * dk + CH * c + CH],
                            W[:, 1024 * dk + 512 * hf: 1024 * dk + 512 * hf + 512],
                            start=(dk == 0), stop=(dk == 7)) for dk in range(8)]
                    return wave
                tk[("wavek0", c)] = sc.add("pe", mkwave_half(WK, 0, 0),
                                           waits=[tk[("gp", c)]])
                tk[("wavek", c)] = sc.add("pe", mkwave_half(WK, 0, 1))

                def mkkss(h0):
                    def kss_ac(s):
                        return [s.activation(
                            scrB[:, 128 * h:128 * h + 128],
                            PSf[:, 128 * h:128 * h + 128],
                            AF.Square, accum_out=qss[:, h:h + 1])
                            for h in range(h0, h0 + 4)]
                    return kss_ac
                tk[("kss0", c)] = sc.add("ac", mkkss(0), waits=[tk[("wavek0", c)]])
                tk[("kss", c)] = sc.add("ac", mkkss(4), waits=[tk[("wavek", c)]])

                def nf1(s):
                    ins = [s.activation(rqk[:], qss[:], AF.Ln, bias=C_e24[:])]
                    s.nop(); s.nop()
                    ins.append(s.activation(rqk[:], rqk[:], AF.Exp, scale=-0.5))
                    return ins
                tk[("nf1", c)] = sc.add("ac", nf1,
                                        waits=[tk[("qssg", c)], tk[("kss0", c)],
                                               tk[("kss", c)], t_smem])

                def nrm_dv(v):
                    r16 = rqk[:].rearrange("p (h o) -> p h o", o=1).broadcast_to([128, 16, 128])
                    return v.tensor_tensor(_hview16(knqn[:]), _hview16(PSf[:, 0:2048]),
                                           r16, OP.mult)
                tk[("qnkn", c)] = sc.add("dv", nrm_dv, waits=[tk[("nf1", c)]])

                def cums_de(t):
                    return [t.matmul(PSf[:, 2048 + 512 * hf:2048 + 512 * hf + 512],
                                     C_mb[:], gp_[:, 512 * hf:512 * hf + 512],
                                     start=True, stop=True) for hf in range(2)]
                CWd = [tk[("gp", c)], tk[("vevac", c)]]
                if c == 0:
                    CWd.append(t_const)
                if c == 8:
                    CWd.append(wmw_tok[0])
                tk[("cumsde", c)] = sc.add("pe", cums_de, waits=CWd)

                def cums(t):
                    ins = []
                    for h in range(8):
                        gph = gp_[:, 128 * h:128 * h + 128]
                        ins.append(t.matmul(
                            PSf[:, 1024 + 128 * h:1024 + 128 * h + 128],
                            gph, C_mac[:, 0:128], start=True, stop=True))
                        ins.append(t.matmul(
                            PSf[:, 2 * h:2 * h + 2],
                            gph, C_mac[:, 128:130], start=True, stop=True))
                    return ins
                CW = [tk[("qnkn", c)]]
                if c == 0:
                    CW.append(t_const)
                tk[("cums", c)] = sc.add("pe", cums, waits=CW)

                def mktransp(h0):
                    base = 256 * h0
                    def transp(t):
                        ins = []
                        for i, h in enumerate(range(h0, h0 + 4)):
                            ins.append(t.transpose(
                                PSb[:, base + 128 * i:base + 128 * i + 128],
                                qn[:, 128 * h:128 * h + 128], C_idn[:]))
                        for i, h in enumerate(range(h0, h0 + 4)):
                            ins.append(t.transpose(
                                PSb[:, base + 512 + 128 * i:base + 512 + 128 * i + 128],
                                kn[:, 128 * h:128 * h + 128], C_idn[:]))
                        return ins
                    return transp
                tk[("transp0", c)] = sc.add("pe", mktransp(0),
                                            waits=[tk[("qnkn", c)],
                                                   tk.get(("qfkf0", c - 1)),
                                                   tk.get(("supd", c - 1))])
                tk[("transp", c)] = sc.add("pe", mktransp(4),
                                           waits=[tk.get(("qfkf", c - 1))])

                def mkexps(hf):
                    a, b = 1024 + 512 * hf, 1536 + 512 * hf
                    def exps(s):
                        return [
                            s.activation(E1a[:, 512 * hf:512 * hf + 512],
                                         PSf[:, a:b], AF.Exp, scale=-1.0),
                            s.activation(scrB[:, 512 * hf:512 * hf + 512],
                                         PSf[:, a:b], AF.Exp),
                        ]
                    return exps
                tk[("exps", c)] = sc.add("ac", mkexps(0),
                                         waits=[tk[("cums", c)],
                                                tk.get(("stilde", c - 1))])
                tk[("exps_1", c)] = sc.add("ac", mkexps(1))
                tk[("exps2", c)] = sc.add("ac", lambda s: [
                    s.activation(E1b[:], PSf[:, 0:16], AF.Exp, scale=-1.0),
                    s.activation(E3[:], PSf[:, 2048:3072], AF.Exp),
                ], waits=[tk[("cums", c)], tk[("cumsde", c)], tk.get(("supd", c - 1)),
                          tk.get(("stilde", c - 1))])

                def mkqfkf(hf):
                    a = 512 * hf
                    base = 1024 * hf
                    def qfkf(v):
                        return [
                            v.tensor_tensor(qf[:, a:a + 512], PSb[:, base:base + 512],
                                            E1a[:, a:a + 512], OP.mult),
                            v.tensor_tensor(kf[:, a:a + 512], PSb[:, base + 512:base + 1024],
                                            scrB[:, a:a + 512], OP.mult),
                        ]
                    return qfkf
                tk[("qfkf0", c)] = sc.add("dv", mkqfkf(0),
                                          waits=[tk[("transp0", c)], tk[("exps", c)]])
                tk[("qfkf", c)] = sc.add("dv", mkqfkf(1),
                                         waits=[tk[("transp", c)], tk[("exps_1", c)]])
                tk[("khat", c)] = sc.add("dv", lambda v: v.tensor_tensor(
                    khat[:], kn, E3[:], OP.mult), waits=[tk[("exps2", c)]])

                def stilde(v):
                    dm = E1b[:].rearrange("p (h two) -> p h two", two=2)[:, :, 1:2]\
                        .broadcast_to([128, 8, 128])
                    return v.tensor_tensor(_hview(St[:]), _hview(S[:]), dm, OP.mult)
                SW = [tk[("exps2", c)], tk.get(("supd", c - 1)), tk.get(("ot", c - 1, 7))]
                if c == 0:
                    SW.append(t_smem)
                tk[("stilde", c)] = sc.add("dv", stilde, waits=SW)

                def em_pt(h):
                    slot = 2048 + 512 * (h % 2) + 176 + 128 * ((h // 2) % 2)
                    pts = PSf[:, slot:slot + 128]
                    W_pt = [tk[("qfkf0" if h < 4 else "qfkf", c)]]
                    if h == 0:
                        W_pt.append(tk.get(("ptm", c - 1, 7)))
                        W_pt.append(tk[("exps2", c)])
                    if h >= 4:
                        W_pt.append(tk[("ptm", c, h - 4)])
                    tk[("pt", c, h)] = sc.add("pe", lambda t, h=h, pts=pts: t.matmul(
                        pts, kf[:, 128 * h:128 * h + 128],
                        qf[:, 128 * h:128 * h + 128], start=True, stop=True),
                        waits=W_pt)
                    W_ptm = [tk[("pt", c, h)]]
                    if c == 0 and h == 0:
                        W_ptm += [t_const, t_smem]
                    tk[("ptm", c, h)] = sc.add("dv", lambda v, h=h, pts=pts: v.copy_predicated(
                        PT[:, 128 * h:128 * h + 128], C_masku[:], pts),
                        waits=W_ptm)

                def em_ot(h):
                    tk[("ot", c, h)] = sc.add("pe", lambda t, h=h: [
                        t.matmul(PSf[:, 1024 + 128 * h:1024 + 128 * h + 128],
                                 vtm[:, 128 * h:128 * h + 128],
                                 PT[:, 128 * h:128 * h + 128], start=True, stop=False),
                        t.matmul(PSf[:, 1024 + 128 * h:1024 + 128 * h + 128],
                                 St[:, 128 * h:128 * h + 128],
                                 qf[:, 128 * h:128 * h + 128], start=False, stop=True),
                    ], waits=[tk[("ptm", c, h)], tk[("stilde", c)], tk[("vevac", c)]])

                def em_ds(h):
                    tk[("ds", c, h)] = sc.add("pe", lambda t, h=h: t.matmul(
                        PSf[:, 128 * h:128 * h + 128],
                        khat[:, 128 * h:128 * h + 128],
                        vtm[:, 128 * h:128 * h + 128], start=True, stop=True),
                        waits=[tk[("khat", c)], tk[("vevac", c)]])

                em_pt(0); em_pt(1)
                for h in range(8):
                    em_ds(h)
                    em_ot(h)
                    if h + 2 < 8:
                        em_pt(h + 2)
                tk[("otev", c)] = sc.add("ac", lambda s: s.copy(
                    OT[:], PSf[:, 1024:2048]), waits=[tk[("ot", c, 7)]])

                def smul(s):
                    return [s.activation(
                        Sm[:, 128 * h:128 * h + 128], S[:, 128 * h:128 * h + 128],
                        AF.Identity, scale=E1b[:, 2 * h:2 * h + 1]) for h in range(8)]
                tk[("smul", c)] = sc.add("ac", smul,
                                         waits=[tk[("exps2", c)], tk[("stilde", c)],
                                                tk.get(("supd", c - 1))])
                tk[("supd", c)] = sc.add("dv", lambda v: v.tensor_tensor(
                    S[:], Sm[:], PSf[:, 0:1024], OP.add),
                    waits=[tk[("ds", c, 7)], tk[("stilde", c)], tk[("smul", c)]])

            def em_tailA(c):
                cm = c % 2

                def wo(t):
                    ins = []
                    for h in range(8):
                        for hf in range(2):
                            ins.append(t.matmul(
                                PSf[:, 1024 + 512 * hf:1024 + 512 * hf + 512],
                                OT[:, 128 * h:128 * h + 128],
                                WO[:, 1024 * h + 512 * hf: 1024 * h + 512 * hf + 512],
                                start=(h == 0), stop=(h == 7)))
                    return ins
                tk[("wo", c)] = sc.add("pe", wo, waits=[tk[("otev", c)]])
                tk[("resid", c)] = sc.add("dv", lambda v: v.scalar_tensor_tensor(
                    x1pre[cm][:], PSf[:, 1024:2048], 1.0, xb[cm][:], OP.mult, OP.add),
                    waits=[tk[("wo", c)], tk[("xin", c)]])

            def em_lnq(c):
                cm = c % 2
                tk[("lnss", c)] = sc.add("ac", lambda s: s.activation(
                    scrB[:], x1pre[cm][:], AF.Square, accum_out=lnst[:, 1:2]),
                    waits=[tk[("resid", c)]])
                tk[("lnst", c)] = sc.add("dv", lambda v: v.tensor_reduce(
                    lnst[:, 0:1], x1pre[cm][:], AX, OP.add), waits=[tk[("resid", c)]])
                ta = sc.add("dv", lambda v: [
                    v.tensor_scalar(lnst[:, 2:3], lnst[:, 0:1], 1.0 / D, None, OP.mult),
                    v.tensor_scalar(lnst[:, 3:4], lnst[:, 1:2], 1.0 / D, None, OP.mult),
                ], waits=[tk[("lnst", c)], tk[("lnss", c)]])
                tb_ = sc.add("dv", lambda v: v.scalar_tensor_tensor(
                    lnst[:, 4:5], lnst[:, 2:3], 1.0, lnst[:, 2:3], OP.mult, OP.mult),
                    waits=[ta])
                tc_ = sc.add("dv", lambda v: v.tensor_tensor(
                    lnst[:, 5:6], lnst[:, 3:4], lnst[:, 4:5], OP.subtract), waits=[tb_])
                tk[("lnf", c)] = sc.add("dv", lambda v: v.tensor_scalar(
                    lnst[:, 6:7], lnst[:, 5:6], LNEPS, None, OP.add), waits=[tc_])

                def lnr(s):
                    ins = [s.activation(lnst[:, 7:8], lnst[:, 6:7], AF.Ln)]
                    s.nop(); s.nop()
                    ins.append(s.activation(lnst[:, 8:9], lnst[:, 7:8], AF.Exp, scale=-0.5))
                    return ins
                tk[("lnr", c)] = sc.add("ac", lnr, waits=[tk[("lnf", c)]])
                tk[("lnb", c)] = sc.add("dv", lambda v: v.scalar_tensor_tensor(
                    lnst[:, 9:10], lnst[:, 2:3], -1.0, lnst[:, 8:9], OP.mult, OP.mult),
                    waits=[tk[("lnr", c)]])
                tk[("lna", c)] = sc.add("ac", lambda s: s.activation(
                    x1b[cm][:], x1pre[cm][:], AF.Identity,
                    bias=lnst[:, 9:10], scale=lnst[:, 8:9]),
                    waits=[tk[("lnb", c)], tk.get(("x1o", c - 2))])
                tk[("qm0", c)] = sc.add("dv", lambda v: v.tensor_reduce(
                    qsc[:, 0:1], x1b[cm][:], AX, OP.max, apply_absolute_value=True),
                    waits=[tk[("lna", c)]])
                tk[("qm", c)] = sc.add("dv", lambda v: v.tensor_scalar(
                    qsc[:, 1:2], qsc[:, 0:1], 1e-5, None, OP.max),
                    waits=[tk[("qm0", c)]])
                tk[("qr", c)] = sc.add("dv", lambda v: v.reciprocal(
                    qsc[:, 2:3], qsc[:, 1:2]), waits=[tk[("qm", c)]])
                tk[("qp", c)] = sc.add("dv", lambda v: [
                    v.tensor_scalar(qsc[:, 3:4], qsc[:, 2:3], 127.0, None, OP.mult),
                    v.tensor_scalar(qsc[:, 4:5], qsc[:, 1:2], 1.0 / 127.0, None, OP.mult),
                ], waits=[tk[("qr", c)]])
                tk[("qa", c)] = sc.add("ac", lambda s: [
                    s.activation(xq8[:], x1b[cm][:], AF.Identity, scale=qsc[:, 3:4]),
                    s.activation(xqs[cm][:], xq8[:], AF.Identity, scale=qsc[:, 4:5]),
                ], waits=[tk[("qp", c)], tk.get(("xqo", c - 2))])
                tk[("x1o", c)] = sc.add("gp", lambda g: g.dma_start(
                    dx1s[CH * c:CH * c + CH, :], x1b[cm][:]),
                    waits=[tk[("lna", c)]], n_dma=1)
                tk[("xqo", c)] = sc.add("gp", lambda g: g.dma_start(
                    dx1q[CH * c:CH * c + CH, :], xqs[cm][:]),
                    waits=[tk[("qa", c)]], n_dma=1)

            mp_i = 0
            qp_i = 0
            for c in range(NCH):
                if c == 8:
                    t_wred = sc.add("dv", lambda v: [
                        v.tensor_reduce(wab[:, 0:1], waccl[:], AX, OP.add),
                        v.tensor_reduce(wab[:, 1:2], wacc2[:], AX, OP.add),
                    ], waits=[t_mred[31], t_mred[63]])
                    t_wsum = sc.add("pe", lambda t: [
                        t.matmul(PSf[0:1, 2048:2049], C_ocf[:], wab[:, 0:1],
                                 start=True, stop=True),
                        t.matmul(PSf[0:1, 2049:2050], C_ocf[:], wab[:, 1:2],
                                 start=True, stop=True),
                    ], waits=[t_wred, tk[("exps2", 7)], tk[("ptm", 7, 7)]])
                    t_wsr = sc.add("ac", lambda s: s.copy(
                        swrow[:], PSf[0:1, 2048:2050]), waits=[t_wsum])
                    t_wbc = sc.add("pe", lambda t: t.matmul(
                        PSf[:, 2052:2054], C_orf[:], swrow[:], start=True, stop=True),
                        waits=[t_wsr])
                    t_wmw0 = sc.add("dv", lambda v: v.tensor_copy(
                        wsc1[:], PSf[:, 2052:2054]), waits=[t_wbc])
                    wmw_tok[0] = sc.add("dv", lambda v: v.tensor_scalar(
                        MWC[:], wsc1[:], 1.0 / (D * DH), 1e-5, OP.mult, OP.max),
                        waits=[t_wmw0])
                    wrec_tok[0] = sc.add("dv", lambda v: v.reciprocal(
                        SRW[:], MWC[:]), waits=[wmw_tok[0]])
                em_headA(c)
                em_tailA(c)
                if c > 0:
                    em_lnq(c - 1)
                if dbg and c == 0:
                    dumps = [
                        sc.add("gp", lambda g, src=src, dst=dst: g.dma_start(
                            dst, src if isinstance(src, bass.AP) else src[:]),
                               waits=[wt], n_dma=1)
                        for src, dst, wt in (
                            (qn, ddbg["dqn"], tk[("qnkn", 0)]),
                            (kn, ddbg["dkn"], tk[("qnkn", 0)]),
                            (vtm, ddbg["dvt"], tk[("vevac", 0)]),
                            (gp_, ddbg["dgp"], tk[("gp", 0)]),
                            (E1a, ddbg["dE1a"], tk[("exps", 0)]),
                            (E1b, ddbg["dE1b"], tk[("exps2", 0)]),
                            (E3, ddbg["dE3"], tk[("exps2", 0)]),
                            (qf, ddbg["dqf"], tk[("qfkf", 0)]),
                            (kf, ddbg["dkf"], tk[("qfkf", 0)]),
                            (PT, ddbg["dPT"], tk[("ptm", 0, 7)]),
                            (OT, ddbg["dOT"], tk[("otev", 0)]),
                            (S, ddbg["dS"], tk[("supd", 0)]),
                        )
                    ]
                    tk[("dbg0",)] = dumps[-1]
                n_pieces = 8
                if c < 8:
                    for _ in range(n_pieces):
                        if mp_i < len(mean_pieces):
                            mean_piece_groups(mp_i)
                            mp_i += 1
                else:
                    for _ in range(n_pieces):
                        if qp_i < len(quant_pieces):
                            quant_piece_groups(qp_i)
                            qp_i += 1
            em_lnq(NCH - 1)
            assert mp_i == len(mean_pieces) and qp_i == len(quant_pieces)
            t_w1q_done = t_qst[31]
            passA_done = [t_qload[63], t_qst[63], tk[("wo", NCH - 1)],
                          t_qq[63], tk[("lna", NCH - 1)], tk[("xqo", NCH - 1)],
                          tk[("x1o", NCH - 1)], tk[("qa", NCH - 1)],
                          tk[("supd", NCH - 1)], tk[("qfkf", NCH - 1)],
                          tk[("ptm", NCH - 1, 7)], tk[("resid", NCH - 1)],
                          tk[("otev", NCH - 1)]]
            sc.flush(block, sems)

        with ExitStack() as pb:
            eb = pb.enter_context
            PSg = eb(nc.psum_tensor("PSfB", [128, 3584], F32))
            PSc = eb(nc.psum_tensor("PSbB", [128, 1024], BF16))
            W1Q = eb(nc.sbuf_tensor("W1Q", [128, 32768], BF16))
            W2Q = eb(nc.sbuf_tensor("W2Q", [128, 32768], BF16))
            XQT = eb(nc.sbuf_tensor("XQT", [128, 8 * (T + 32)], BF16))
            H2 = eb(nc.sbuf_tensor("H2", [128, NHC * TB], BF16))
            cbuf = [eb(nc.sbuf_tensor(f"cbuf{i}", [128, TB + 2], BF16)) for i in range(2)]
            tc1 = [eb(nc.sbuf_tensor(f"tc1{i}", [128, TB], BF16)) for i in range(2)]
            tc2 = [eb(nc.sbuf_tensor(f"tc2{i}", [128, TB], BF16)) for i in range(2)]
            Mp = eb(nc.sbuf_tensor("Mp", [128, TB], BF16))
            Mn = eb(nc.sbuf_tensor("Mn", [128, TB], BF16))
            mtok = eb(nc.sbuf_tensor("mtok", [128, 16], F32))
            rt16 = eb(nc.sbuf_tensor("rt16", [128, 2], BF16))
            rrow = eb(nc.sbuf_tensor("rrow", [1, 256], BF16))
            srb = eb(nc.sbuf_tensor("srb", [128, TB], BF16))
            HQ8 = eb(nc.sbuf_tensor("HQ8", [128, TB], I8))
            HQB = [eb(nc.sbuf_tensor(f"HQB{i}", [128, TB], BF16)) for i in range(2)]
            outsb = [eb(nc.sbuf_tensor(f"outs{i}", [128, D], F32)) for i in range(2)]
            bpre = [eb(nc.sbuf_tensor(f"bpre{i}", [128, D], F32)) for i in range(2)]
            blst = eb(nc.sbuf_tensor("blst", [128, 12], F32))

            btk = {}
            t_xqtz = sc.add("dv", lambda v: v.memset(XQT[:], 0.0), waits=passA_done)

            def xqt_loads(s):
                return [s.dma_start(
                    XQT[:, (T + 32) * j + 16: (T + 32) * j + 16 + T],
                    dx1q[:, 128 * j:128 * j + 128], transpose=True) for j in range(8)]
            t_xqt = sc.add("sy", xqt_loads, waits=passA_done + [t_xqtz], n_dma=8)
            t_w1l = sc.add("gp", lambda g: g.dma_start(
                W1Q[:], dw1q.rearrange("(a p) f -> p a f", p=128)),
                waits=passA_done + [t_w1q_done], n_dma=1)
            t_w2l = sc.add("gp", lambda g: g.dma_start(
                W2Q[:], dw2q.rearrange("(a p) f -> p a f", p=128)),
                waits=passA_done, n_dma=1)

            def em_x1l(tb):
                t0 = TB * tb
                for half in range(2):
                    btk[("x1l", tb, half)] = sc.add("sy", lambda s, half=half, t0=t0: s.dma_start(
                        bpre[half][:], dx1s[t0 + 128 * half: t0 + 128 * half + 128, :]),
                        waits=[btk.get(("lna2", tb - 1, half))], n_dma=1)

            def em_conv(tb, g):
                t0 = TB * tb
                off = 512 * (g % 3)
                def hmm(t, g=g, off=off, t0=t0):
                    return [t.matmul(
                        PSg[:, off: off + TB + 2],
                        W1Q[:, 4096 * dk + 128 * g: 4096 * dk + 128 * g + 128],
                        XQT[:, (T + 32) * dk + 15 + t0: (T + 32) * dk + 15 + t0 + TB + 2],
                        start=(dk == 0), stop=(dk == 7)) for dk in range(8)]
                Wh = [t_xqt, t_w1l, btk.get(("cevac", tb, g - 3))]
                if tb > 0 and g < 3:
                    Wh.append(btk.get(("cevac", tb - 1, (30, 31, 29)[g])))
                if tb == 0 and g == 0:
                    Wh += passA_done
                btk[("hmm", tb, g)] = sc.add("pe", hmm, waits=Wh)
                btk[("cevac", tb, g)] = sc.add("ac", lambda s, off=off, g=g: s.activation(
                    cbuf[g % 2][:], PSg[:, off: off + TB + 2],
                    AF.Identity, scale=MWC[:, 0:1]),
                    waits=[btk[("hmm", tb, g)], btk.get(("conv", tb, g - 2)),
                           (btk.get(("conv", tb - 1, NHC - 2 + g)) if tb > 0 and g < 2 else None)])
                def conv(v, g=g):
                    cb_ = cbuf[g % 2]
                    t1 = tc1[g % 2]
                    t2 = tc2[g % 2]
                    return [
                        v.tensor_scalar(t1[:], cb_[:, 0:TB], C_cw[:, 3 * g:3 * g + 1],
                                        None, OP.mult),
                        v.scalar_tensor_tensor(t2[:], cb_[:, 1:TB + 1],
                                               C_cw[:, 3 * g + 1:3 * g + 2], t1[:],
                                               OP.mult, OP.add),
                        v.scalar_tensor_tensor(t1[:], cb_[:, 2:TB + 2],
                                               C_cw[:, 3 * g + 2:3 * g + 3], t2[:],
                                               OP.mult, OP.add),
                    ]
                btk[("conv", tb, g)] = sc.add("dv", conv, waits=[
                    btk[("cevac", tb, g)], btk.get(("silu", tb, g - 2)),
                    (btk.get(("silu", tb - 1, NHC - 2 + g)) if tb > 0 and g < 2 else None)])
                btk[("silu", tb, g)] = sc.add("ac", lambda s, g=g: s.activation(
                    H2[:, TB * g:TB * g + TB], tc1[g % 2][:], AF.Silu,
                    bias=C_cb[:, g:g + 1]),
                    waits=[btk[("conv", tb, g)], btk.get(("q8", tb - 1, g)),
                           (btk.get(("dbg2",)) if tb == 1 else None)])
                def mm(v, g=g):
                    h2g = H2[:, TB * g:TB * g + TB]
                    if g == 0:
                        return [v.tensor_copy(Mp[:], h2g), v.tensor_copy(Mn[:], h2g)]
                    return [v.tensor_tensor(Mp[:], Mp[:], h2g, OP.max),
                            v.tensor_tensor(Mn[:], Mn[:], h2g, OP.min)]
                btk[("mm", tb, g)] = sc.add("dv", mm, waits=[
                    btk[("silu", tb, g)],
                    (btk.get(("mtr", tb - 1)) if g == 0 else None)])

            def em_ladder(tb):
                mb_ = 8 * (tb % 2)
                btk[("mfin", tb)] = sc.add("dv", lambda v: [
                    v.tensor_scalar(Mn[:], Mn[:], -1.0, None, OP.mult),
                    v.tensor_tensor(Mp[:], Mp[:], Mn[:], OP.max),
                ], waits=[btk[("mm", tb, NHC - 1)]])
                btk[("mtr", tb)] = sc.add("pe", lambda t: [
                    t.transpose(PSc[:, 0:128], Mp[:, 0:128], C_idn[:]),
                    t.transpose(PSc[:, 128:256], Mp[:, 128:256], C_idn[:]),
                ], waits=[btk[("mfin", tb)], btk.get(("mrow", tb - 1))])
                btk[("mred0", tb)] = sc.add("dv", lambda v: [
                    v.tensor_reduce(mtok[:, mb_ + 0:mb_ + 1], PSc[:, 0:128], AX, OP.max),
                    v.tensor_reduce(mtok[:, mb_ + 1:mb_ + 2], PSc[:, 128:256], AX, OP.max),
                ], waits=[btk[("mtr", tb)]])
                btk[("mred", tb)] = sc.add("dv", lambda v: v.tensor_scalar(
                    mtok[:, mb_ + 2:mb_ + 4], mtok[:, mb_ + 0:mb_ + 2], 1e-5, None, OP.max),
                    waits=[btk[("mred0", tb)]])
                btk[("mrec", tb)] = sc.add("dv", lambda v: v.reciprocal(
                    mtok[:, mb_ + 4:mb_ + 6], mtok[:, mb_ + 2:mb_ + 4]),
                    waits=[btk[("mred", tb)]])
                btk[("mr2", tb)] = sc.add("dv", lambda v: v.tensor_scalar(
                    rt16[:], mtok[:, mb_ + 4:mb_ + 6], 127.0, None, OP.mult),
                    waits=[btk[("mrec", tb)]])
                btk[("mtr2", tb)] = sc.add("pe", lambda t: [
                    t.transpose(PSc[0:1, 256:384], rt16[:, 0:1], C_idn[:]),
                    t.transpose(PSc[0:1, 384:512], rt16[:, 1:2], C_idn[:]),
                ], waits=[btk[("mr2", tb)]])
                btk[("mrow", tb)] = sc.add("ac", lambda s: [
                    s.copy(rrow[0:1, 0:128], PSc[0:1, 256:384]),
                    s.copy(rrow[0:1, 128:256], PSc[0:1, 384:512]),
                ], waits=[btk[("mtr2", tb)]])
                btk[("srbc", tb)] = sc.add("pe", lambda t: t.matmul(
                    PSg[:, 1536:1792], C_o1b[:], rrow[0:1, 0:256],
                    start=True, stop=True),
                    waits=[btk[("mrow", tb)], btk.get(("q8", tb - 1, NHC - 1)),
                           btk.get(("y2ev", tb - 1, 0))])
                btk[("srev", tb)] = sc.add("dv", lambda v: v.tensor_copy(
                    srb[:], PSg[:, 1536:1792]), waits=[btk[("srbc", tb)]])

            def em_q8y2(tb, g):
                btk[("q8", tb, g)] = sc.add("dv", lambda v, g=g: [
                    v.tensor_tensor(HQ8[:], H2[:, TB * g:TB * g + TB],
                                    srb[:], OP.mult),
                    v.tensor_copy(HQB[g % 2][:], HQ8[:]),
                ], waits=[btk[("srev", tb)], btk.get(("y2", tb, g - 2))])
                def y2(t, g=g):
                    ins = []
                    for half in range(2):
                        base = 1536 + 1024 * half
                        for nh in range(2):
                            ins.append(t.matmul(
                                PSg[:, base + 512 * nh: base + 512 * nh + 512],
                                HQB[g % 2][:, 128 * half:128 * half + 128],
                                W2Q[:, 1024 * g + 512 * nh: 1024 * g + 512 * nh + 512],
                                start=(g == 0), stop=(g == NHC - 1)))
                    return ins
                btk[("y2", tb, g)] = sc.add("pe", y2, waits=[
                    btk[("q8", tb, g)],
                    (t_w2l if tb == 0 and g == 0 else None),
                    (btk.get(("y2ev", tb - 1, 1)) if g == 0 else None)])

            def em_y2ev(tb):
                mb_ = 8 * (tb % 2)
                for half in range(2):
                    base = 1536 + 1024 * half
                    btk[("dq", tb, half)] = sc.add("dv", lambda v, half=half: v.tensor_scalar(
                        blst[:, 10 + half:11 + half], mtok[:, mb_ + 2 + half:mb_ + 3 + half],
                        MWC[:, 1:2], 1.0 / 127.0, OP.mult, OP.mult),
                        waits=[btk[("mred", tb)], wmw_tok[0]])
                    btk[("y2ev", tb, half)] = sc.add(
                        "dv", lambda v, half=half, base=base: v.scalar_tensor_tensor(
                            bpre[half][:], PSg[:, base:base + 1024],
                            blst[:, 10 + half:11 + half],
                            bpre[half][:], OP.mult, OP.add),
                        waits=[btk[("y2", tb, NHC - 1)], btk[("dq", tb, half)],
                               btk[("x1l", tb, 1)]])

            def em_tailLN(tb):
                t0 = TB * tb
                for half in range(2):
                    btk[("lnss2", tb, half)] = sc.add("ac", lambda s, half=half: s.activation(
                        outsb[half][:], bpre[half][:], AF.Square, accum_out=blst[:, 1:2]),
                        waits=[btk[("y2ev", tb, half)],
                               btk.get(("outd", tb - 1, half))])
                    btk[("lnst2", tb, half)] = sc.add("dv", lambda v, half=half: v.tensor_reduce(
                        blst[:, 0:1], bpre[half][:], AX, OP.add),
                        waits=[btk[("y2ev", tb, half)]])
                    ta2 = sc.add("dv", lambda v: [
                        v.tensor_scalar(blst[:, 2:3], blst[:, 0:1], 1.0 / D, None, OP.mult),
                        v.tensor_scalar(blst[:, 3:4], blst[:, 1:2], 1.0 / D, None, OP.mult),
                    ], waits=[btk[("lnst2", tb, half)], btk[("lnss2", tb, half)]])
                    tb2 = sc.add("dv", lambda v: v.scalar_tensor_tensor(
                        blst[:, 4:5], blst[:, 2:3], 1.0, blst[:, 2:3], OP.mult, OP.mult),
                        waits=[ta2])
                    tc2_ = sc.add("dv", lambda v: v.tensor_tensor(
                        blst[:, 5:6], blst[:, 3:4], blst[:, 4:5], OP.subtract), waits=[tb2])
                    btk[("lnf2", tb, half)] = sc.add("dv", lambda v: v.tensor_scalar(
                        blst[:, 6:7], blst[:, 5:6], LNEPS, None, OP.add), waits=[tc2_])
                    def lnr2(s):
                        ins = [s.activation(blst[:, 7:8], blst[:, 6:7], AF.Ln)]
                        s.nop(); s.nop()
                        ins.append(s.activation(blst[:, 8:9], blst[:, 7:8], AF.Exp, scale=-0.5))
                        return ins
                    btk[("lnr2", tb, half)] = sc.add("ac", lnr2, waits=[btk[("lnf2", tb, half)]])
                    btk[("lnb2", tb, half)] = sc.add("dv", lambda v: v.scalar_tensor_tensor(
                        blst[:, 9:10], blst[:, 2:3], -1.0, blst[:, 8:9], OP.mult, OP.mult),
                        waits=[btk[("lnr2", tb, half)]])
                    btk[("lna2", tb, half)] = sc.add("ac", lambda s, half=half: s.activation(
                        outsb[half][:], bpre[half][:], AF.Identity,
                        bias=blst[:, 9:10], scale=blst[:, 8:9]),
                        waits=[btk[("lnb2", tb, half)], btk.get(("outd", tb - 1, half)),
                               btk[("lnss2", tb, half)]])
                    btk[("outd", tb, half)] = sc.add("sy", lambda s, half=half, t0=t0: s.dma_start(
                        dout[t0 + 128 * half: t0 + 128 * half + 128, :], outsb[half][:]),
                        waits=[btk[("lna2", tb, half)]], n_dma=1)

            for tb in range(NTB + 1):
                if dbg and tb == 1:
                    d2 = [
                        sc.add("gp", lambda g, src=src, dst=dst: g.dma_start(dst, src),
                               waits=[wt], n_dma=1)
                        for src, dst, wt in (
                            (H2[:], ddbg["dH2"], btk[("mm", 0, NHC - 1)]),
                            (mtok[:, 0:8], ddbg["dmt"], btk[("mr2", 0)]),
                            (MWC[:], ddbg["dMWC"], wmw_tok[0]),
                            (XQT[:], ddbg["dXQT"], t_xqt),
                        )
                    ]
                    btk[("dbg2",)] = d2[-1]
                if dbg and tb == 2:
                    sc.add("gp", lambda g: g.dma_start(ddbg["dsrb1"], srb[:]),
                           waits=[btk[("srev", 1)]], n_dma=1)
                    sc.add("gp", lambda g: g.dma_start(ddbg["dmt1"], mtok[:, 8:16]),
                           waits=[btk[("mrec", 1)]], n_dma=1)
                if dbg and tb == 3:
                    sc.add("gp", lambda g: g.dma_start(ddbg["dbp1"], bpre[0][:]),
                           waits=[btk[("y2ev", 1, 0)]], n_dma=1)
                if tb >= 1:
                    for g in range(NHC):
                        em_q8y2(tb - 1, g)
                if tb >= 2:
                    em_tailLN(tb - 2)
                if tb >= 1:
                    em_x1l(tb - 1)
                    em_y2ev(tb - 1)
                if tb < NTB:
                    for g in range(NHC):
                        em_conv(tb, g)
                    em_ladder(tb)
            em_tailLN(NTB - 1)
            sc.add("sy", lambda s: s.nop(),
                   waits=[btk[("outd", NTB - 1, 0)], btk[("outd", NTB - 1, 1)]])
            sc.flush(block, sems)

    return nc


def _np_consts():
    tri = np.triu(np.ones((128, 128), np.float32))
    le63 = (np.arange(128) <= 63).astype(np.float32)
    Ma = tri - le63[:, None]
    c1 = np.ones((128, 1), np.float32)
    c2 = le63[:, None].copy()
    mac = np.ascontiguousarray(
        np.concatenate([Ma, c1, c2], axis=1)).astype(ml_dtypes.bfloat16)
    mb = np.ascontiguousarray(-(1.0 - tri)).astype(ml_dtypes.bfloat16)
    mask = np.ascontiguousarray(
        np.triu(np.ones((128, 128), np.float32))).astype(ml_dtypes.bfloat16)
    idn = np.eye(128, dtype=ml_dtypes.bfloat16)
    return mac, mb, mask, idn


_CACHE = {}


def _get_nc(dbg=False):
    key = ("nc", dbg)
    if key not in _CACHE:
        _CACHE[key] = build_program(dbg)
    return _CACHE[key]


def _common_inputs(Wq, Wk, Wv, Wg, Wo, w1, conv_w, conv_b, w2):
    f32 = lambda a: np.ascontiguousarray(np.asarray(a, np.float32))
    mac, mb, mask, idn = _np_consts()
    cw = f32(conv_w).reshape(DH, 3).reshape(NHC, 128, 3).transpose(1, 0, 2)
    cw = np.ascontiguousarray(cw.reshape(128, 3 * NHC))
    cb = np.ascontiguousarray(f32(conv_b).reshape(NHC, 128).T)
    return dict(
        wq=f32(Wq), wk=f32(Wk), wv=f32(Wv), wg=f32(Wg), wo=f32(Wo),
        w1t=np.ascontiguousarray(f32(w1).T), w2t=np.ascontiguousarray(f32(w2).T),
        cw=cw, cb=cb, idn=idn, mac=mac, mbm=mb, mask=mask,
        masku=np.ascontiguousarray(np.triu(np.ones((128, 128), np.uint8))),
        ones1b=np.ones((1, 128), ml_dtypes.bfloat16),
        onescf=np.ones((128, 1), np.float32),
        onesrf=np.ones((1, 128), np.float32),
    )


def kernel(L0_attn_x, L0_attn_Wq, L0_attn_Wk, L0_attn_Wv, L0_attn_Wg, L0_attn_Wo,
           L0_ffn_w1, L0_ffn_conv_w, L0_ffn_conv_b, L0_ffn_w2):
    x = np.ascontiguousarray(np.asarray(L0_attn_x, np.float32))
    common = _common_inputs(L0_attn_Wq, L0_attn_Wk, L0_attn_Wv, L0_attn_Wg,
                            L0_attn_Wo, L0_ffn_w1, L0_ffn_conv_w, L0_ffn_conv_b,
                            L0_ffn_w2)
    in_maps = []
    for b in range(8):
        m = dict(common)
        m["x"] = np.ascontiguousarray(x[b])
        m["xt"] = np.ascontiguousarray(x[b].T)
        in_maps.append(m)
    res = run_bass_kernel_spmd(_get_nc(), in_maps, core_ids=list(range(8)))
    return np.stack([r["out"] for r in res.results], axis=0)


# version 16
# speedup vs baseline: 1.0212x; 1.0212x over previous
import numpy as np
import ml_dtypes
from contextlib import ExitStack

import concourse.bass as bass
import concourse.mybir as mybir
from concourse.bass_utils import run_bass_kernel_spmd

F32 = mybir.dt.float32
BF16 = mybir.dt.bfloat16
I8 = mybir.dt.int8
OP = mybir.AluOpType
AF = mybir.ActivationFunctionType
AX = mybir.AxisListType.X

T, D, H, DK = 2048, 1024, 8, 128
NCH, CH = 16, 128
DH, NHC = 4096, 32
TB, NTB = 256, 8
LNEPS = 1e-5

ENGS = ("sy", "gp", "pe", "dv", "ac")


class Sched:

    def __init__(self):
        self.pending = {e: [] for e in ENGS}
        self.cnt = {e: 0 for e in ENGS}

    def add(self, eng, emit, waits=(), n_dma=0):
        amt = 16 * n_dma if n_dma else 1
        self.cnt[eng] += amt
        tok = (eng, self.cnt[eng])
        ws = {}
        for w in waits:
            if w is None:
                continue
            we, wv = w
            ws[we] = max(ws.get(we, 0), wv)
        self.pending[eng].append((ws, emit, n_dma))
        return tok

    def flush(self, block, sems):
        bodies = {"sy": block.sync, "gp": block.gpsimd, "pe": block.tensor,
                  "dv": block.vector, "ac": block.scalar}
        for eng_name, groups in self.pending.items():
            if not groups:
                continue
            def body(eng, groups=groups, eng_name=eng_name):
                for ws, emit, n_dma in groups:
                    for we, wv in sorted(ws.items()):
                        eng.wait_ge(sems[we], wv)
                    insts = emit(eng)
                    if not isinstance(insts, (list, tuple)):
                        insts = [insts]
                    if n_dma:
                        assert len(insts) == n_dma
                        for i in insts:
                            i.then_inc(sems[eng_name], 16)
                    else:
                        insts[-1].then_inc(sems[eng_name], 1)
            bodies[eng_name](body)
        self.pending = {e: [] for e in ENGS}


def _hview(ap):
    return ap.rearrange("p (h f) -> p h f", h=8)


def build_program(dbg=False):
    nc = bass.Bass("TRN2", target_bir_lowering=False)

    dx = nc.dram_tensor("x", [T, D], F32, kind="ExternalInput").ap()
    dxt = nc.dram_tensor("xt", [D, T], F32, kind="ExternalInput").ap()
    dwq = nc.dram_tensor("wq", [D, D], F32, kind="ExternalInput").ap()
    dwk = nc.dram_tensor("wk", [D, D], F32, kind="ExternalInput").ap()
    dwv = nc.dram_tensor("wv", [D, D], F32, kind="ExternalInput").ap()
    dwg = nc.dram_tensor("wg", [D, D], F32, kind="ExternalInput").ap()
    dwo = nc.dram_tensor("wo", [D, D], F32, kind="ExternalInput").ap()
    dw1t = nc.dram_tensor("w1t", [D, DH], F32, kind="ExternalInput").ap()
    dw2t = nc.dram_tensor("w2t", [DH, D], F32, kind="ExternalInput").ap()
    dcw = nc.dram_tensor("cw", [128, 3 * NHC], F32, kind="ExternalInput").ap()
    dcb = nc.dram_tensor("cb", [128, NHC], F32, kind="ExternalInput").ap()
    didn = nc.dram_tensor("idn", [128, 128], BF16, kind="ExternalInput").ap()
    dmac = nc.dram_tensor("mac", [128, 130], BF16, kind="ExternalInput").ap()
    dmb = nc.dram_tensor("mbm", [128, 128], BF16, kind="ExternalInput").ap()
    dmask = nc.dram_tensor("mask", [128, 128], BF16, kind="ExternalInput").ap()
    dmasku = nc.dram_tensor("masku", [128, 128], mybir.dt.uint8, kind="ExternalInput").ap()
    dones1b = nc.dram_tensor("ones1b", [1, 128], BF16, kind="ExternalInput").ap()
    donescf = nc.dram_tensor("onescf", [128, 1], F32, kind="ExternalInput").ap()
    donesrf = nc.dram_tensor("onesrf", [1, 128], F32, kind="ExternalInput").ap()

    scratch_kind = "ExternalOutput" if dbg else "Internal"
    ddbg = {}
    if dbg:
        for nm, shp, dt in (("dqn", [128, D], BF16), ("dkn", [128, D], BF16),
                            ("dvt", [128, D], BF16), ("dgp", [128, D], BF16),
                            ("dE1a", [128, D], F32), ("dE1b", [128, 16], F32),
                            ("dE2", [128, D], F32), ("dE3", [128, D], BF16),
                            ("dqf", [128, D], BF16), ("dkf", [128, D], BF16),
                            ("dPT", [128, D], BF16), ("dOT", [128, D], BF16),
                            ("dxp", [128, D], F32), ("dS", [128, D], BF16),
                            ("dxb", [128, D], F32), ("dlnst", [128, 10], F32),
                            ("dq8", [128, D], F32), ("dqss", [128, 16], F32),
                            ("dH2", [128, NHC * TB], BF16), ("dmt", [128, 8], F32),
                            ("dy2", [128, D], F32), ("dMWC", [128, 2], F32),
                            ("dXQT", [128, 8 * (T + 32)], BF16)):
            ddbg[nm] = nc.dram_tensor(nm, shp, dt, kind="ExternalOutput").ap()
    dx1s = nc.dram_tensor("x1s", [T, D], F32, kind=scratch_kind).ap()
    dx1q = nc.dram_tensor("x1qd", [T, D], BF16, kind=scratch_kind).ap()
    dw1q = nc.dram_tensor("w1qd", [D, DH], I8, kind=scratch_kind).ap()
    dw2q = nc.dram_tensor("w2qd", [DH, D], I8, kind=scratch_kind).ap()
    dout = nc.dram_tensor("out", [T, D], F32, kind="ExternalOutput").ap()

    sc = Sched()
    tk = {}

    with ExitStack() as top:
        e = top.enter_context
        sems = {n: e(nc.semaphore("sem_" + n)) for n in ENGS}
        block = e(nc.Block())
        C_idn = e(nc.sbuf_tensor("C_idn", [128, 128], BF16))
        C_mac = e(nc.sbuf_tensor("C_mac", [128, 130], BF16))
        C_mb = e(nc.sbuf_tensor("C_mb", [128, 128], BF16))
        C_mask = e(nc.sbuf_tensor("C_mask", [128, 128], BF16))
        C_masku = e(nc.sbuf_tensor("C_masku", [128, 128], mybir.dt.uint8))
        C_o1b = e(nc.sbuf_tensor("C_o1b", [1, 128], BF16))
        C_ocf = e(nc.sbuf_tensor("C_ocf", [128, 1], F32))
        C_orf = e(nc.sbuf_tensor("C_orf", [1, 128], F32))
        C_cw = e(nc.sbuf_tensor("C_cw", [128, 3 * NHC], F32))
        C_cb = e(nc.sbuf_tensor("C_cb", [128, NHC], F32))
        waccl = e(nc.sbuf_tensor("waccl", [128, 32], F32))
        wacc2 = e(nc.sbuf_tensor("wacc2", [128, 32], F32))
        wab = e(nc.sbuf_tensor("wab", [128, 2], F32))
        swrow = e(nc.sbuf_tensor("swrow", [1, 2], F32))
        MWC = e(nc.sbuf_tensor("MWC", [128, 2], F32))
        SRW = e(nc.sbuf_tensor("SRW", [128, 2], F32))
        wsc1 = e(nc.sbuf_tensor("wsc1", [128, 2], F32))
        C_e24 = e(nc.sbuf_tensor("C_e24", [128, 1], F32))

        t_const = sc.add("sy", lambda s: [
            s.dma_start(C_idn[:], didn), s.dma_start(C_mac[:], dmac),
            s.dma_start(C_mb[:], dmb), s.dma_start(C_mask[:], dmask),
            s.dma_start(C_o1b[:], dones1b), s.dma_start(C_ocf[:], donescf),
            s.dma_start(C_orf[:], donesrf), s.dma_start(C_cw[:], dcw),
            s.dma_start(C_cb[:], dcb),
            s.dma_start(C_masku[:], dmasku),
        ], n_dma=10)

        with ExitStack() as pa:
            ea = pa.enter_context
            PSf = ea(nc.psum_tensor("PSfA", [128, 3072], F32))
            PSb = ea(nc.psum_tensor("PSbA", [128, 2048], BF16))
            WQ = ea(nc.sbuf_tensor("WQ", [128, 8192], BF16))
            WK = ea(nc.sbuf_tensor("WK", [128, 8192], BF16))
            WV = ea(nc.sbuf_tensor("WV", [128, 8192], BF16))
            WG = ea(nc.sbuf_tensor("WG", [128, 8192], BF16))
            WO = ea(nc.sbuf_tensor("WO", [128, 8192], BF16))
            XT = ea(nc.sbuf_tensor("XT", [128, 16384], BF16))
            xb = [ea(nc.sbuf_tensor(f"xb{i}", [128, D], F32)) for i in range(2)]
            qn = ea(nc.sbuf_tensor("qn", [128, D], BF16))
            kn = ea(nc.sbuf_tensor("kn", [128, D], BF16))
            vtm = ea(nc.sbuf_tensor("vtm", [128, D], BF16))
            khat = ea(nc.sbuf_tensor("khat", [128, D], BF16))
            gp_ = ea(nc.sbuf_tensor("gp", [128, D], BF16))
            E1a = ea(nc.sbuf_tensor("E1a", [128, D], F32))
            E1b = ea(nc.sbuf_tensor("E1b", [128, 16], F32))
            E3 = ea(nc.sbuf_tensor("E3", [128, D], BF16))
            qf = ea(nc.sbuf_tensor("qf", [128, D], BF16))
            kf = ea(nc.sbuf_tensor("kf", [128, D], BF16))
            PT = ea(nc.sbuf_tensor("PT", [128, D], BF16))
            OT = ea(nc.sbuf_tensor("OT", [128, D], BF16))
            S = ea(nc.sbuf_tensor("S", [128, D], BF16))
            St = ea(nc.sbuf_tensor("St", [128, D], BF16))
            Sm = ea(nc.sbuf_tensor("Sm", [128, D], BF16))
            scrA = ea(nc.sbuf_tensor("scrA", [128, D], F32))
            scrB = ea(nc.sbuf_tensor("scrB", [128, D], F32))
            qss = ea(nc.sbuf_tensor("qss", [128, 16], F32))
            rqk = ea(nc.sbuf_tensor("rqk", [128, 16], F32))
            x1pre = [ea(nc.sbuf_tensor(f"x1pre{i}", [128, D], F32)) for i in range(2)]
            x1b = [ea(nc.sbuf_tensor(f"x1b{i}", [128, D], F32)) for i in range(2)]
            lnst = ea(nc.sbuf_tensor("lnst", [128, 10], F32))
            xq8 = ea(nc.sbuf_tensor("xq8", [128, D], I8))
            xqs = [ea(nc.sbuf_tensor(f"xqs{i}", [128, D], BF16)) for i in range(2)]
            wstage = [ea(nc.sbuf_tensor(f"wstage{i}", [128, 1024], F32)) for i in range(4)]
            wscr = ea(nc.sbuf_tensor("wscr", [128, 1024], F32))
            wq8 = [ea(nc.sbuf_tensor(f"wq8{i}", [128, 1024], I8)) for i in range(4)]

            def wload(g):
                ins = []
                for dst, src in ((WQ, dwq), (WK, dwk), (WV, dwv), (WG, dwg), (WO, dwo)):
                    ins.append(g.dma_start(dst[:], src.rearrange("(a p) f -> p a f", p=128)))
                ins.append(g.dma_start(XT[:], dxt.rearrange("(a p) t -> p a t", p=128)))
                return ins
            t_wload = sc.add("gp", wload, n_dma=6)
            t_smem = sc.add("dv", lambda v: [v.memset(S[:], 0.0),
                                             v.memset(PT[:], 0.0),
                                             v.memset(C_e24[:], 1e-24)])

            mean_pieces = [("w1", a, b) for a in range(8) for b in range(4)] + \
                          [("w2", g, 0) for g in range(32)]
            quant_pieces = list(mean_pieces)
            t_mload, t_mred, t_qload, t_qq, t_qst = {}, {}, {}, {}, {}
            wrec_tok = [None]
            wmw_tok = [None]

            def mean_piece_groups(i):
                which, a, b = mean_pieces[i]
                buf = wstage[i % 4]
                if which == "w1":
                    src = dw1t[128 * a:128 * a + 128, 1024 * b:1024 * b + 1024]
                else:
                    src = dw2t[128 * a:128 * a + 128, 0:D]
                t_mload[i] = sc.add("sy", lambda s, buf=buf, src=src:
                                    s.dma_start(buf[:], src),
                                    waits=[t_mred.get(i - 4)], n_dma=1)
                if which == "w1":
                    t_mred[i] = sc.add("dv", lambda v, buf=buf, i=i:
                                       v.tensor_reduce(waccl[:, i:i + 1], buf[:], AX,
                                                       OP.add, apply_absolute_value=True),
                                       waits=[t_mload[i]])
                else:
                    j = i - 32
                    t_mred[i] = sc.add("ac", lambda s, buf=buf, j=j:
                                       s.activation(wscr[:], buf[:], AF.Abs,
                                                    accum_out=wacc2[:, j:j + 1]),
                                       waits=[t_mload[i]])

            def quant_piece_groups(i):
                which, a, b = quant_pieces[i]
                buf = wstage[i % 4]
                q8 = wq8[i % 4]
                if which == "w1":
                    src = dw1t[128 * a:128 * a + 128, 1024 * b:1024 * b + 1024]
                    dst = dw1q[128 * a:128 * a + 128, 1024 * b:1024 * b + 1024]
                    scol = SRW[:, 0:1]
                else:
                    src = dw2t[128 * a:128 * a + 128, 0:D]
                    dst = dw2q[128 * a:128 * a + 128, 0:D]
                    scol = SRW[:, 1:2]
                t_qload[i] = sc.add("sy", lambda s, buf=buf, src=src:
                                    s.dma_start(buf[:], src),
                                    waits=[t_qq.get(i - 4), wrec_tok[0]], n_dma=1)
                t_qq[i] = sc.add("dv", lambda v, buf=buf, q8=q8, scol=scol: [
                    v.tensor_scalar(wscr[:], buf[:], scol, 1.0, OP.mult, OP.min),
                    v.tensor_scalar(q8[:], wscr[:], -1.0, None, OP.max),
                ], waits=[t_qload[i], t_qst.get(i - 4)])
                t_qst[i] = sc.add("gp", lambda g, dst=dst, q8=q8:
                                  g.dma_start(dst, q8[:]),
                                  waits=[t_qq[i]], n_dma=1)

            def fb(bank, off, n):
                return PSf[:, 512 * bank + off: 512 * bank + off + n]

            qsc = ea(nc.sbuf_tensor("qsc", [128, 6], F32))

            def em_headA(c):
                cm = c % 2
                tk[("xin", c)] = sc.add("sy", lambda s: s.dma_start(
                    xb[cm][:], dx[CH * c:CH * c + CH, :]),
                    waits=[tk.get(("resid", c - 2))], n_dma=1)

                def mkwave(W, base):
                    def wave(t):
                        ins = []
                        for dk in range(8):
                            xts = XT[:, 2048 * dk + CH * c: 2048 * dk + CH * c + CH]
                            for hf in range(2):
                                ins.append(t.matmul(
                                    PSf[:, base + 512 * hf: base + 512 * hf + 512], xts,
                                    W[:, 1024 * dk + 512 * hf: 1024 * dk + 512 * hf + 512],
                                    start=(dk == 0), stop=(dk == 7)))
                        return ins
                    return wave
                Wv = [tk.get(("ptm", c - 1, 7)), t_wload, t_const]
                if c == 8:
                    Wv.append(wmw_tok[0])
                if ("dbg0",) in tk and c == 1:
                    Wv.append(tk[("dbg0",)])
                tk[("wavev", c)] = sc.add("pe", mkwave(WV, 2048), waits=Wv)
                tk[("vevac", c)] = sc.add("dv", lambda v: v.tensor_copy(
                    vtm[:], PSf[:, 2048:3072]), waits=[tk[("wavev", c)]])
                tk[("wavez", c)] = sc.add("pe", mkwave(WG, 0),
                                          waits=[tk.get(("supd", c - 1))])
                tk[("gp", c)] = sc.add("ac", lambda s: [
                    s.activation(scrB[:], PSf[:, 0:1024], AF.Exp, scale=-1.0),
                    s.activation(gp_[:], scrB[:], AF.Ln, bias=1.0),
                ], waits=[tk[("wavez", c)], tk.get(("qfkf", c - 1)),
                          tk.get(("khat", c - 1))])
                tk[("waveq", c)] = sc.add("pe", mkwave(WQ, 1024),
                                          waits=[tk.get(("resid", c - 1))])

                def qss_ac(s):
                    return [s.activation(
                        scrB[:, 128 * h:128 * h + 128],
                        PSf[:, 1024 + 128 * h:1024 + 128 * h + 128],
                        AF.Square, accum_out=qss[:, h:h + 1]) for h in range(8)]
                tk[("qssg", c)] = sc.add("ac", qss_ac, waits=[tk[("waveq", c)]])
                tk[("wavek", c)] = sc.add("pe", mkwave(WK, 0), waits=[tk[("gp", c)]])

                def kss_ac(s):
                    return [s.activation(
                        scrB[:, 128 * h:128 * h + 128],
                        PSf[:, 128 * h:128 * h + 128],
                        AF.Square, accum_out=qss[:, 8 + h:9 + h]) for h in range(8)]
                tk[("kss", c)] = sc.add("ac", kss_ac, waits=[tk[("wavek", c)]])

                def nf1(s):
                    ins = [s.activation(rqk[:], qss[:], AF.Ln, bias=C_e24[:])]
                    s.nop(); s.nop()
                    ins.append(s.activation(rqk[:], rqk[:], AF.Exp, scale=-0.5))
                    return ins
                tk[("nf1", c)] = sc.add("ac", nf1,
                                        waits=[tk[("qssg", c)], tk[("kss", c)], t_smem])

                def qn_dv(v):
                    rq = rqk[:, 0:8].rearrange("p (h o) -> p h o", o=1).broadcast_to([128, 8, 128])
                    return v.tensor_tensor(_hview(qn[:]), _hview(PSf[:, 1024:2048]),
                                           rq, OP.mult)
                tk[("qng", c)] = sc.add("dv", qn_dv, waits=[tk[("nf1", c)]])

                def kn_dv(v):
                    rk = rqk[:, 8:16].rearrange("p (h o) -> p h o", o=1).broadcast_to([128, 8, 128])
                    return v.tensor_tensor(_hview(kn[:]), _hview(PSf[:, 0:1024]), rk, OP.mult)
                tk[("qnkn", c)] = sc.add("dv", kn_dv, waits=[tk[("nf1", c)]])

                def cums(t):
                    ins = []
                    for hf in range(2):
                        ins.append(t.matmul(PSf[:, 2048 + 512 * hf:2048 + 512 * hf + 512],
                                            C_mb[:], gp_[:, 512 * hf:512 * hf + 512],
                                            start=True, stop=True))
                    for h in range(8):
                        gph = gp_[:, 128 * h:128 * h + 128]
                        ins.append(t.matmul(
                            PSf[:, 1024 + 128 * h:1024 + 128 * h + 128],
                            gph, C_mac[:, 0:128], start=True, stop=True))
                        ins.append(t.matmul(
                            PSf[:, 2 * h:2 * h + 2],
                            gph, C_mac[:, 128:130], start=True, stop=True))
                    return ins
                CW = [tk[("vevac", c)], tk[("qng", c)], tk[("kss", c)]]
                if c == 0:
                    CW.append(t_const)
                if c == 8:
                    CW.append(wmw_tok[0])
                tk[("cums", c)] = sc.add("pe", cums, waits=CW)

                def transp(t):
                    ins = []
                    for h in range(8):
                        ins.append(t.transpose(PSb[:, 128 * h:128 * h + 128],
                                               qn[:, 128 * h:128 * h + 128], C_idn[:]))
                    for h in range(8):
                        ins.append(t.transpose(PSb[:, 1024 + 128 * h:1024 + 128 * h + 128],
                                               kn[:, 128 * h:128 * h + 128], C_idn[:]))
                    return ins
                tk[("transp", c)] = sc.add("pe", transp,
                                           waits=[tk[("qng", c)], tk[("qnkn", c)],
                                                  tk.get(("qfkf", c - 1)),
                                                  tk.get(("supd", c - 1))])
                tk[("exps2a", c)] = sc.add("ac", lambda s: s.activation(
                    E1b[:], PSf[:, 0:16], AF.Exp, scale=-1.0),
                    waits=[tk[("cums", c)], tk.get(("supd", c - 1)),
                           tk.get(("stilde", c - 1))])
                tk[("exps", c)] = sc.add("ac", lambda s: [
                    s.activation(E1a[:], PSf[:, 1024:2048], AF.Exp, scale=-1.0),
                    s.activation(scrB[:], PSf[:, 1024:2048], AF.Exp),
                ], waits=[tk[("cums", c)], tk.get(("stilde", c - 1))])
                tk[("exps2", c)] = sc.add("ac", lambda s: s.activation(
                    E3[:], PSf[:, 2048:3072], AF.Exp),
                    waits=[tk[("cums", c)], tk.get(("supd", c - 1)),
                           tk.get(("stilde", c - 1))])
                tk[("qfkf", c)] = sc.add("dv", lambda v: [
                    v.tensor_tensor(qf[:], PSb[:, 0:1024], E1a[:], OP.mult),
                    v.tensor_tensor(kf[:], PSb[:, 1024:2048], scrB[:], OP.mult),
                ], waits=[tk[("transp", c)], tk[("exps", c)]])
                tk[("khat", c)] = sc.add("dv", lambda v: v.tensor_tensor(
                    khat[:], kn[:], E3[:], OP.mult), waits=[tk[("exps2", c)]])

                def stilde(v):
                    dm = E1b[:].rearrange("p (h two) -> p h two", two=2)[:, :, 1:2]\
                        .broadcast_to([128, 8, 128])
                    return v.tensor_tensor(_hview(St[:]), _hview(S[:]), dm, OP.mult)
                SW = [tk[("exps2a", c)], tk.get(("supd", c - 1)), tk.get(("ot", c - 1, 7))]
                if c == 0:
                    SW.append(t_smem)
                tk[("stilde", c)] = sc.add("dv", stilde, waits=SW)

                def em_pt(h):
                    slot = 2048 + 512 * (h % 2) + 176 + 128 * ((h // 2) % 2)
                    pts = PSf[:, slot:slot + 128]
                    W_pt = [tk[("qfkf", c)]]
                    if h == 0:
                        W_pt.append(tk.get(("ptm", c - 1, 7)))
                    if h >= 4:
                        W_pt.append(tk[("ptm", c, h - 4)])
                    tk[("pt", c, h)] = sc.add("pe", lambda t, h=h, pts=pts: t.matmul(
                        pts, kf[:, 128 * h:128 * h + 128],
                        qf[:, 128 * h:128 * h + 128], start=True, stop=True),
                        waits=W_pt)
                    W_ptm = [tk[("pt", c, h)]]
                    if c == 0 and h == 0:
                        W_ptm += [t_const, t_smem]
                    tk[("ptm", c, h)] = sc.add("dv", lambda v, h=h, pts=pts: v.copy_predicated(
                        PT[:, 128 * h:128 * h + 128], C_masku[:], pts),
                        waits=W_ptm)

                def em_ot(h):
                    tk[("ot", c, h)] = sc.add("pe", lambda t, h=h: [
                        t.matmul(PSf[:, 1024 + 128 * h:1024 + 128 * h + 128],
                                 vtm[:, 128 * h:128 * h + 128],
                                 PT[:, 128 * h:128 * h + 128], start=True, stop=False),
                        t.matmul(PSf[:, 1024 + 128 * h:1024 + 128 * h + 128],
                                 St[:, 128 * h:128 * h + 128],
                                 qf[:, 128 * h:128 * h + 128], start=False, stop=True),
                    ], waits=[tk[("ptm", c, h)], tk[("stilde", c)], tk[("vevac", c)]])

                def em_ds(h):
                    tk[("ds", c, h)] = sc.add("pe", lambda t, h=h: t.matmul(
                        PSf[:, 128 * h:128 * h + 128],
                        khat[:, 128 * h:128 * h + 128],
                        vtm[:, 128 * h:128 * h + 128], start=True, stop=True),
                        waits=[tk[("khat", c)], tk[("vevac", c)], tk[("qfkf", c)]])

                em_pt(0); em_pt(1)
                for h in range(8):
                    em_ds(h)
                    em_ot(h)
                    if h + 2 < 8:
                        em_pt(h + 2)
                tk[("otev", c)] = sc.add("ac", lambda s: s.copy(
                    OT[:], PSf[:, 1024:2048]), waits=[tk[("ot", c, 7)]])

                def smul(s):
                    return [s.activation(
                        Sm[:, 128 * h:128 * h + 128], S[:, 128 * h:128 * h + 128],
                        AF.Identity, scale=E1b[:, 2 * h:2 * h + 1]) for h in range(8)]
                tk[("smul", c)] = sc.add("ac", smul,
                                         waits=[tk[("exps2a", c)], tk[("stilde", c)],
                                                tk.get(("supd", c - 1))])
                tk[("supd", c)] = sc.add("dv", lambda v: v.tensor_tensor(
                    S[:], Sm[:], PSf[:, 0:1024], OP.add),
                    waits=[tk[("ds", c, 7)], tk[("stilde", c)], tk[("smul", c)]])

            def em_tailA(c):
                cm = c % 2

                def wo(t):
                    ins = []
                    for h in range(8):
                        for hf in range(2):
                            ins.append(t.matmul(
                                PSf[:, 1024 + 512 * hf:1024 + 512 * hf + 512],
                                OT[:, 128 * h:128 * h + 128],
                                WO[:, 1024 * h + 512 * hf: 1024 * h + 512 * hf + 512],
                                start=(h == 0), stop=(h == 7)))
                    return ins
                tk[("wo", c)] = sc.add("pe", wo, waits=[tk[("otev", c)]])
                tk[("resid", c)] = sc.add("dv", lambda v: v.scalar_tensor_tensor(
                    x1pre[cm][:], PSf[:, 1024:2048], 1.0, xb[cm][:], OP.mult, OP.add),
                    waits=[tk[("wo", c)], tk[("xin", c)]])

            def em_lnq(c):
                cm = c % 2
                tk[("lnss", c)] = sc.add("ac", lambda s: s.activation(
                    scrB[:], x1pre[cm][:], AF.Square, accum_out=lnst[:, 1:2]),
                    waits=[tk[("resid", c)]])
                tk[("lnst", c)] = sc.add("dv", lambda v: v.tensor_reduce(
                    lnst[:, 0:1], x1pre[cm][:], AX, OP.add), waits=[tk[("resid", c)]])
                ta = sc.add("dv", lambda v: [
                    v.tensor_scalar(lnst[:, 2:3], lnst[:, 0:1], 1.0 / D, None, OP.mult),
                    v.tensor_scalar(lnst[:, 3:4], lnst[:, 1:2], 1.0 / D, None, OP.mult),
                ], waits=[tk[("lnst", c)], tk[("lnss", c)]])
                tb_ = sc.add("dv", lambda v: v.scalar_tensor_tensor(
                    lnst[:, 4:5], lnst[:, 2:3], 1.0, lnst[:, 2:3], OP.mult, OP.mult),
                    waits=[ta])
                tc_ = sc.add("dv", lambda v: v.tensor_tensor(
                    lnst[:, 5:6], lnst[:, 3:4], lnst[:, 4:5], OP.subtract), waits=[tb_])
                tk[("lnf", c)] = sc.add("dv", lambda v: v.tensor_scalar(
                    lnst[:, 6:7], lnst[:, 5:6], LNEPS, None, OP.add), waits=[tc_])

                def lnr(s):
                    ins = [s.activation(lnst[:, 7:8], lnst[:, 6:7], AF.Ln)]
                    s.nop(); s.nop()
                    ins.append(s.activation(lnst[:, 8:9], lnst[:, 7:8], AF.Exp, scale=-0.5))
                    return ins
                tk[("lnr", c)] = sc.add("ac", lnr, waits=[tk[("lnf", c)]])
                tk[("lnb", c)] = sc.add("dv", lambda v: v.scalar_tensor_tensor(
                    lnst[:, 9:10], lnst[:, 2:3], -1.0, lnst[:, 8:9], OP.mult, OP.mult),
                    waits=[tk[("lnr", c)]])
                tk[("lna", c)] = sc.add("ac", lambda s: s.activation(
                    x1b[cm][:], x1pre[cm][:], AF.Identity,
                    bias=lnst[:, 9:10], scale=lnst[:, 8:9]),
                    waits=[tk[("lnb", c)], tk.get(("x1o", c - 2))])
                tk[("qm0", c)] = sc.add("dv", lambda v: v.tensor_reduce(
                    qsc[:, 0:1], x1b[cm][:], AX, OP.max, apply_absolute_value=True),
                    waits=[tk[("lna", c)]])
                tk[("qm", c)] = sc.add("dv", lambda v: v.tensor_scalar(
                    qsc[:, 1:2], qsc[:, 0:1], 1e-5, None, OP.max),
                    waits=[tk[("qm0", c)]])
                tk[("qr", c)] = sc.add("dv", lambda v: v.reciprocal(
                    qsc[:, 2:3], qsc[:, 1:2]), waits=[tk[("qm", c)]])
                tk[("qp", c)] = sc.add("dv", lambda v: [
                    v.tensor_scalar(qsc[:, 3:4], qsc[:, 2:3], 127.0, None, OP.mult),
                    v.tensor_scalar(qsc[:, 4:5], qsc[:, 1:2], 1.0 / 127.0, None, OP.mult),
                ], waits=[tk[("qr", c)]])
                tk[("qa", c)] = sc.add("ac", lambda s: [
                    s.activation(xq8[:], x1b[cm][:], AF.Identity, scale=qsc[:, 3:4]),
                    s.activation(xqs[cm][:], xq8[:], AF.Identity, scale=qsc[:, 4:5]),
                ], waits=[tk[("qp", c)], tk.get(("xqo", c - 2))])
                tk[("x1o", c)] = sc.add("gp", lambda g: g.dma_start(
                    dx1s[CH * c:CH * c + CH, :], x1b[cm][:]),
                    waits=[tk[("lna", c)]], n_dma=1)
                tk[("xqo", c)] = sc.add("gp", lambda g: g.dma_start(
                    dx1q[CH * c:CH * c + CH, :], xqs[cm][:]),
                    waits=[tk[("qa", c)]], n_dma=1)

            mp_i = 0
            qp_i = 0
            for c in range(NCH):
                if c == 8:
                    t_wred = sc.add("dv", lambda v: [
                        v.tensor_reduce(wab[:, 0:1], waccl[:], AX, OP.add),
                        v.tensor_reduce(wab[:, 1:2], wacc2[:], AX, OP.add),
                    ], waits=[t_mred[31], t_mred[63]])
                    t_wsum = sc.add("pe", lambda t: [
                        t.matmul(PSf[0:1, 2048:2049], C_ocf[:], wab[:, 0:1],
                                 start=True, stop=True),
                        t.matmul(PSf[0:1, 2049:2050], C_ocf[:], wab[:, 1:2],
                                 start=True, stop=True),
                    ], waits=[t_wred, tk[("exps2", 7)], tk[("ptm", 7, 7)]])
                    t_wsr = sc.add("ac", lambda s: s.copy(
                        swrow[:], PSf[0:1, 2048:2050]), waits=[t_wsum])
                    t_wbc = sc.add("pe", lambda t: t.matmul(
                        PSf[:, 2052:2054], C_orf[:], swrow[:], start=True, stop=True),
                        waits=[t_wsr])
                    t_wmw0 = sc.add("dv", lambda v: v.tensor_copy(
                        wsc1[:], PSf[:, 2052:2054]), waits=[t_wbc])
                    wmw_tok[0] = sc.add("dv", lambda v: v.tensor_scalar(
                        MWC[:], wsc1[:], 1.0 / (D * DH), 1e-5, OP.mult, OP.max),
                        waits=[t_wmw0])
                    wrec_tok[0] = sc.add("dv", lambda v: v.reciprocal(
                        SRW[:], MWC[:]), waits=[wmw_tok[0]])
                em_headA(c)
                em_tailA(c)
                if c > 0:
                    em_lnq(c - 1)
                if dbg and c == 0:
                    dumps = [
                        sc.add("gp", lambda g, src=src, dst=dst: g.dma_start(dst, src[:]),
                               waits=[wt], n_dma=1)
                        for src, dst, wt in (
                            (qn, ddbg["dqn"], tk[("qng", 0)]),
                            (kn, ddbg["dkn"], tk[("qnkn", 0)]),
                            (vtm, ddbg["dvt"], tk[("vevac", 0)]),
                            (gp_, ddbg["dgp"], tk[("gp", 0)]),
                            (E1a, ddbg["dE1a"], tk[("exps", 0)]),
                            (E1b, ddbg["dE1b"], tk[("exps2a", 0)]),
                            (E3, ddbg["dE3"], tk[("exps2", 0)]),
                            (qf, ddbg["dqf"], tk[("qfkf", 0)]),
                            (kf, ddbg["dkf"], tk[("qfkf", 0)]),
                            (PT, ddbg["dPT"], tk[("ptm", 0, 7)]),
                            (OT, ddbg["dOT"], tk[("otev", 0)]),
                            (S, ddbg["dS"], tk[("supd", 0)]),
                        )
                    ]
                    tk[("dbg0",)] = dumps[-1]
                n_pieces = 8
                if c < 8:
                    for _ in range(n_pieces):
                        if mp_i < len(mean_pieces):
                            mean_piece_groups(mp_i)
                            mp_i += 1
                else:
                    for _ in range(n_pieces):
                        if qp_i < len(quant_pieces):
                            quant_piece_groups(qp_i)
                            qp_i += 1
            em_lnq(NCH - 1)
            assert mp_i == len(mean_pieces) and qp_i == len(quant_pieces)
            t_w1q_done = t_qst[31]
            passA_done = [t_qload[63], t_qst[63], tk[("wo", NCH - 1)],
                          t_qq[63], tk[("lna", NCH - 1)], tk[("xqo", NCH - 1)],
                          tk[("x1o", NCH - 1)], tk[("qa", NCH - 1)],
                          tk[("supd", NCH - 1)], tk[("qfkf", NCH - 1)],
                          tk[("ptm", NCH - 1, 7)], tk[("resid", NCH - 1)],
                          tk[("otev", NCH - 1)]]
            sc.flush(block, sems)

        with ExitStack() as pb:
            eb = pb.enter_context
            PSg = eb(nc.psum_tensor("PSfB", [128, 3584], F32))
            PSc = eb(nc.psum_tensor("PSbB", [128, 1024], BF16))
            W1Q = eb(nc.sbuf_tensor("W1Q", [128, 32768], BF16))
            W2Q = eb(nc.sbuf_tensor("W2Q", [128, 32768], BF16))
            XQT = eb(nc.sbuf_tensor("XQT", [128, 8 * (T + 32)], BF16))
            H2 = eb(nc.sbuf_tensor("H2", [128, NHC * TB], BF16))
            cbuf = [eb(nc.sbuf_tensor(f"cbuf{i}", [128, TB + 2], BF16)) for i in range(2)]
            tc1 = [eb(nc.sbuf_tensor(f"tc1{i}", [128, TB], BF16)) for i in range(2)]
            tc2 = [eb(nc.sbuf_tensor(f"tc2{i}", [128, TB], BF16)) for i in range(2)]
            Mp = eb(nc.sbuf_tensor("Mp", [128, TB], BF16))
            Mn = eb(nc.sbuf_tensor("Mn", [128, TB], BF16))
            mtok = eb(nc.sbuf_tensor("mtok", [128, 8], F32))
            rt16 = eb(nc.sbuf_tensor("rt16", [128, 2], BF16))
            rrow = eb(nc.sbuf_tensor("rrow", [1, 256], BF16))
            srb = eb(nc.sbuf_tensor("srb", [128, TB], BF16))
            HQ8 = eb(nc.sbuf_tensor("HQ8", [128, TB], I8))
            HQB = [eb(nc.sbuf_tensor(f"HQB{i}", [128, TB], BF16)) for i in range(2)]
            x1t = [eb(nc.sbuf_tensor(f"x1t{i}", [128, D], F32)) for i in range(2)]
            outsb = [eb(nc.sbuf_tensor(f"outs{i}", [128, D], F32)) for i in range(2)]
            bpre = eb(nc.sbuf_tensor("bpre", [128, D], F32))
            blst = eb(nc.sbuf_tensor("blst", [128, 12], F32))

            btk = {}
            t_xqtz = sc.add("dv", lambda v: v.memset(XQT[:], 0.0), waits=passA_done)

            def xqt_loads(s):
                return [s.dma_start(
                    XQT[:, (T + 32) * j + 16: (T + 32) * j + 16 + T],
                    dx1q[:, 128 * j:128 * j + 128], transpose=True) for j in range(8)]
            t_xqt = sc.add("sy", xqt_loads, waits=passA_done + [t_xqtz], n_dma=8)
            t_w1l = sc.add("gp", lambda g: g.dma_start(
                W1Q[:], dw1q.rearrange("(a p) f -> p a f", p=128)),
                waits=passA_done + [t_w1q_done], n_dma=1)
            t_w2l = sc.add("gp", lambda g: g.dma_start(
                W2Q[:], dw2q.rearrange("(a p) f -> p a f", p=128)),
                waits=passA_done, n_dma=1)

            def em_x1l(tb):
                t0 = TB * tb
                for half in range(2):
                    btk[("x1l", tb, half)] = sc.add("sy", lambda s, half=half, t0=t0: s.dma_start(
                        x1t[half][:], dx1s[t0 + 128 * half: t0 + 128 * half + 128, :]),
                        waits=[btk.get(("y2ev", tb - 1, half))], n_dma=1)

            def em_conv(tb, g):
                t0 = TB * tb
                bank = g % 2
                def hmm(t, g=g, bank=bank, t0=t0):
                    return [t.matmul(
                        PSg[:, 512 * bank: 512 * bank + TB + 2],
                        W1Q[:, 4096 * dk + 128 * g: 4096 * dk + 128 * g + 128],
                        XQT[:, (T + 32) * dk + 15 + t0: (T + 32) * dk + 15 + t0 + TB + 2],
                        start=(dk == 0), stop=(dk == 7)) for dk in range(8)]
                Wh = [t_xqt, t_w1l, btk.get(("cevac", tb, g - 2))]
                if tb > 0 and g < 2:
                    Wh.append(btk.get(("cevac", tb - 1, NHC - 2 + g)))
                if tb == 0 and g == 0:
                    Wh += passA_done
                btk[("hmm", tb, g)] = sc.add("pe", hmm, waits=Wh)
                btk[("cevac", tb, g)] = sc.add("ac", lambda s, bank=bank, g=g: s.activation(
                    cbuf[g % 2][:], PSg[:, 512 * bank: 512 * bank + TB + 2],
                    AF.Identity, scale=MWC[:, 0:1]),
                    waits=[btk[("hmm", tb, g)], btk.get(("conv", tb, g - 2)),
                           (btk.get(("conv", tb - 1, NHC - 2 + g)) if tb > 0 and g < 2 else None)])
                def conv(v, g=g):
                    cb_ = cbuf[g % 2]
                    t1 = tc1[g % 2]
                    t2 = tc2[g % 2]
                    return [
                        v.tensor_scalar(t1[:], cb_[:, 0:TB], C_cw[:, 3 * g:3 * g + 1],
                                        None, OP.mult),
                        v.scalar_tensor_tensor(t2[:], cb_[:, 1:TB + 1],
                                               C_cw[:, 3 * g + 1:3 * g + 2], t1[:],
                                               OP.mult, OP.add),
                        v.scalar_tensor_tensor(t1[:], cb_[:, 2:TB + 2],
                                               C_cw[:, 3 * g + 2:3 * g + 3], t2[:],
                                               OP.mult, OP.add),
                    ]
                btk[("conv", tb, g)] = sc.add("dv", conv, waits=[
                    btk[("cevac", tb, g)], btk.get(("silu", tb, g - 2)),
                    (btk.get(("silu", tb - 1, NHC - 2 + g)) if tb > 0 and g < 2 else None)])
                btk[("silu", tb, g)] = sc.add("ac", lambda s, g=g: s.activation(
                    H2[:, TB * g:TB * g + TB], tc1[g % 2][:], AF.Silu,
                    bias=C_cb[:, g:g + 1]),
                    waits=[btk[("conv", tb, g)], btk.get(("q8", tb - 1, g)),
                           (btk.get(("dbg2",)) if tb == 1 else None)])
                def mm(v, g=g):
                    h2g = H2[:, TB * g:TB * g + TB]
                    if g == 0:
                        return [v.tensor_copy(Mp[:], h2g), v.tensor_copy(Mn[:], h2g)]
                    return [v.tensor_tensor(Mp[:], Mp[:], h2g, OP.max),
                            v.tensor_tensor(Mn[:], Mn[:], h2g, OP.min)]
                btk[("mm", tb, g)] = sc.add("dv", mm, waits=[
                    btk[("silu", tb, g)],
                    (btk.get(("mtr", tb - 1)) if g == 0 else None)])

            def em_ladder(tb):
                btk[("mfin", tb)] = sc.add("dv", lambda v: [
                    v.tensor_scalar(Mn[:], Mn[:], -1.0, None, OP.mult),
                    v.tensor_tensor(Mp[:], Mp[:], Mn[:], OP.max),
                ], waits=[btk[("mm", tb, NHC - 1)]])
                btk[("mtr", tb)] = sc.add("pe", lambda t: [
                    t.transpose(PSc[:, 0:128], Mp[:, 0:128], C_idn[:]),
                    t.transpose(PSc[:, 128:256], Mp[:, 128:256], C_idn[:]),
                ], waits=[btk[("mfin", tb)], btk.get(("mrow", tb - 1))])
                btk[("mred0", tb)] = sc.add("dv", lambda v: [
                    v.tensor_reduce(mtok[:, 0:1], PSc[:, 0:128], AX, OP.max),
                    v.tensor_reduce(mtok[:, 1:2], PSc[:, 128:256], AX, OP.max),
                ], waits=[btk[("mtr", tb)]])
                btk[("mred", tb)] = sc.add("dv", lambda v: v.tensor_scalar(
                    mtok[:, 2:4], mtok[:, 0:2], 1e-5, None, OP.max),
                    waits=[btk[("mred0", tb)]])
                btk[("mrec", tb)] = sc.add("dv", lambda v: v.reciprocal(
                    mtok[:, 4:6], mtok[:, 2:4]), waits=[btk[("mred", tb)]])
                btk[("mr2", tb)] = sc.add("dv", lambda v: v.tensor_scalar(
                    rt16[:], mtok[:, 4:6], 127.0, None, OP.mult),
                    waits=[btk[("mrec", tb)]])
                btk[("mtr2", tb)] = sc.add("pe", lambda t: [
                    t.transpose(PSc[0:1, 256:384], rt16[:, 0:1], C_idn[:]),
                    t.transpose(PSc[0:1, 384:512], rt16[:, 1:2], C_idn[:]),
                ], waits=[btk[("mr2", tb)]])
                btk[("mrow", tb)] = sc.add("ac", lambda s: [
                    s.copy(rrow[0:1, 0:128], PSc[0:1, 256:384]),
                    s.copy(rrow[0:1, 128:256], PSc[0:1, 384:512]),
                ], waits=[btk[("mtr2", tb)]])
                btk[("srbc", tb)] = sc.add("pe", lambda t: [
                    t.matmul(PSg[:, 1024:1152], C_o1b[:], rrow[0:1, 0:128],
                             start=True, stop=True),
                    t.matmul(PSg[:, 1152:1280], C_o1b[:], rrow[0:1, 128:256],
                             start=True, stop=True),
                ], waits=[btk[("mrow", tb)], btk.get(("q8", tb - 1, NHC - 1))])
                btk[("srev", tb)] = sc.add("dv", lambda v: v.tensor_copy(
                    srb[:], PSg[:, 1024:1280]), waits=[btk[("srbc", tb)]])

            def em_q8y2(tb, g):
                btk[("q8", tb, g)] = sc.add("dv", lambda v, g=g: [
                    v.tensor_tensor(HQ8[:], H2[:, TB * g:TB * g + TB],
                                    srb[:], OP.mult),
                    v.tensor_copy(HQB[g % 2][:], HQ8[:]),
                ], waits=[btk[("srev", tb)], btk.get(("y2", tb, g - 2))])
                def y2(t, g=g):
                    ins = []
                    for half in range(2):
                        base = 1536 + 1024 * half
                        for nh in range(2):
                            ins.append(t.matmul(
                                PSg[:, base + 512 * nh: base + 512 * nh + 512],
                                HQB[g % 2][:, 128 * half:128 * half + 128],
                                W2Q[:, 1024 * g + 512 * nh: 1024 * g + 512 * nh + 512],
                                start=(g == 0), stop=(g == NHC - 1)))
                    return ins
                btk[("y2", tb, g)] = sc.add("pe", y2, waits=[
                    btk[("q8", tb, g)],
                    (t_w2l if tb == 0 and g == 0 else None),
                    (btk.get(("y2ev", tb - 1, 1)) if g == 0 else None)])

            def em_tail(tb):
                t0 = TB * tb
                for half in range(2):
                    base = 1536 + 1024 * half
                    btk[("dq", tb, half)] = sc.add("dv", lambda v, half=half: v.tensor_scalar(
                        blst[:, 10 + half:11 + half], mtok[:, 2 + half:3 + half],
                        MWC[:, 1:2], 1.0 / 127.0, OP.mult, OP.mult),
                        waits=[btk[("mred", tb)], wmw_tok[0]])
                    btk[("y2ev", tb, half)] = sc.add(
                        "dv", lambda v, half=half, base=base: v.scalar_tensor_tensor(
                            bpre[:], PSg[:, base:base + 1024],
                            blst[:, 10 + half:11 + half],
                            x1t[half][:], OP.mult, OP.add),
                        waits=[btk[("y2", tb, NHC - 1)], btk[("dq", tb, half)],
                               btk[("x1l", tb, half)],
                               (btk.get(("lna2", tb, 0)) if half == 1 else None),
                               (btk.get(("lna2", tb - 1, 1)) if half == 0 else None)])
                    btk[("lnss2", tb, half)] = sc.add("ac", lambda s, half=half: s.activation(
                        outsb[half][:], bpre[:], AF.Square, accum_out=blst[:, 1:2]),
                        waits=[btk[("y2ev", tb, half)],
                               btk.get(("outd", tb - 1, half))])
                    btk[("lnst2", tb, half)] = sc.add("dv", lambda v, half=half: v.tensor_reduce(
                        blst[:, 0:1], bpre[:], AX, OP.add),
                        waits=[btk[("y2ev", tb, half)]])
                    ta2 = sc.add("dv", lambda v: [
                        v.tensor_scalar(blst[:, 2:3], blst[:, 0:1], 1.0 / D, None, OP.mult),
                        v.tensor_scalar(blst[:, 3:4], blst[:, 1:2], 1.0 / D, None, OP.mult),
                    ], waits=[btk[("lnst2", tb, half)], btk[("lnss2", tb, half)]])
                    tb2 = sc.add("dv", lambda v: v.scalar_tensor_tensor(
                        blst[:, 4:5], blst[:, 2:3], 1.0, blst[:, 2:3], OP.mult, OP.mult),
                        waits=[ta2])
                    tc2_ = sc.add("dv", lambda v: v.tensor_tensor(
                        blst[:, 5:6], blst[:, 3:4], blst[:, 4:5], OP.subtract), waits=[tb2])
                    btk[("lnf2", tb, half)] = sc.add("dv", lambda v: v.tensor_scalar(
                        blst[:, 6:7], blst[:, 5:6], LNEPS, None, OP.add), waits=[tc2_])
                    def lnr2(s):
                        ins = [s.activation(blst[:, 7:8], blst[:, 6:7], AF.Ln)]
                        s.nop(); s.nop()
                        ins.append(s.activation(blst[:, 8:9], blst[:, 7:8], AF.Exp, scale=-0.5))
                        return ins
                    btk[("lnr2", tb, half)] = sc.add("ac", lnr2, waits=[btk[("lnf2", tb, half)]])
                    btk[("lnb2", tb, half)] = sc.add("dv", lambda v: v.scalar_tensor_tensor(
                        blst[:, 9:10], blst[:, 2:3], -1.0, blst[:, 8:9], OP.mult, OP.mult),
                        waits=[btk[("lnr2", tb, half)]])
                    btk[("lna2", tb, half)] = sc.add("ac", lambda s, half=half: s.activation(
                        outsb[half][:], bpre[:], AF.Identity,
                        bias=blst[:, 9:10], scale=blst[:, 8:9]),
                        waits=[btk[("lnb2", tb, half)], btk.get(("outd", tb - 1, half)),
                               btk[("lnss2", tb, half)]])
                    btk[("outd", tb, half)] = sc.add("sy", lambda s, half=half, t0=t0: s.dma_start(
                        dout[t0 + 128 * half: t0 + 128 * half + 128, :], outsb[half][:]),
                        waits=[btk[("lna2", tb, half)]], n_dma=1)

            for tb in range(NTB + 1):
                if dbg and tb == 1:
                    d2 = [
                        sc.add("gp", lambda g, src=src, dst=dst: g.dma_start(dst, src),
                               waits=[wt], n_dma=1)
                        for src, dst, wt in (
                            (H2[:], ddbg["dH2"], btk[("mm", 0, NHC - 1)]),
                            (mtok[:], ddbg["dmt"], btk[("mr2", 0)]),
                            (MWC[:], ddbg["dMWC"], wmw_tok[0]),
                            (XQT[:], ddbg["dXQT"], t_xqt),
                        )
                    ]
                    btk[("dbg2",)] = d2[-1]
                if tb >= 1:
                    for g in range(NHC):
                        em_q8y2(tb - 1, g)
                if tb < NTB:
                    for g in range(NHC):
                        em_conv(tb, g)
                if tb >= 1:
                    em_tail(tb - 1)
                if tb < NTB:
                    em_x1l(tb)
                    em_ladder(tb)
            sc.add("sy", lambda s: s.nop(),
                   waits=[btk[("outd", NTB - 1, 0)], btk[("outd", NTB - 1, 1)]])
            sc.flush(block, sems)

    return nc


def _np_consts():
    tri = np.triu(np.ones((128, 128), np.float32))
    le63 = (np.arange(128) <= 63).astype(np.float32)
    Ma = tri - le63[:, None]
    c1 = np.ones((128, 1), np.float32)
    c2 = le63[:, None].copy()
    mac = np.ascontiguousarray(
        np.concatenate([Ma, c1, c2], axis=1)).astype(ml_dtypes.bfloat16)
    mb = np.ascontiguousarray(-(1.0 - tri)).astype(ml_dtypes.bfloat16)
    mask = np.ascontiguousarray(
        np.triu(np.ones((128, 128), np.float32))).astype(ml_dtypes.bfloat16)
    idn = np.eye(128, dtype=ml_dtypes.bfloat16)
    return mac, mb, mask, idn


_CACHE = {}


def _get_nc(dbg=False):
    key = ("nc", dbg)
    if key not in _CACHE:
        _CACHE[key] = build_program(dbg)
    return _CACHE[key]


def _common_inputs(Wq, Wk, Wv, Wg, Wo, w1, conv_w, conv_b, w2):
    f32 = lambda a: np.ascontiguousarray(np.asarray(a, np.float32))
    mac, mb, mask, idn = _np_consts()
    cw = f32(conv_w).reshape(DH, 3).reshape(NHC, 128, 3).transpose(1, 0, 2)
    cw = np.ascontiguousarray(cw.reshape(128, 3 * NHC))
    cb = np.ascontiguousarray(f32(conv_b).reshape(NHC, 128).T)
    return dict(
        wq=f32(Wq), wk=f32(Wk), wv=f32(Wv), wg=f32(Wg), wo=f32(Wo),
        w1t=np.ascontiguousarray(f32(w1).T), w2t=np.ascontiguousarray(f32(w2).T),
        cw=cw, cb=cb, idn=idn, mac=mac, mbm=mb, mask=mask,
        masku=np.ascontiguousarray(np.triu(np.ones((128, 128), np.uint8))),
        ones1b=np.ones((1, 128), ml_dtypes.bfloat16),
        onescf=np.ones((128, 1), np.float32),
        onesrf=np.ones((1, 128), np.float32),
    )


def kernel(L0_attn_x, L0_attn_Wq, L0_attn_Wk, L0_attn_Wv, L0_attn_Wg, L0_attn_Wo,
           L0_ffn_w1, L0_ffn_conv_w, L0_ffn_conv_b, L0_ffn_w2):
    x = np.ascontiguousarray(np.asarray(L0_attn_x, np.float32))
    common = _common_inputs(L0_attn_Wq, L0_attn_Wk, L0_attn_Wv, L0_attn_Wg,
                            L0_attn_Wo, L0_ffn_w1, L0_ffn_conv_w, L0_ffn_conv_b,
                            L0_ffn_w2)
    in_maps = []
    for b in range(8):
        m = dict(common)
        m["x"] = np.ascontiguousarray(x[b])
        m["xt"] = np.ascontiguousarray(x[b].T)
        in_maps.append(m)
    res = run_bass_kernel_spmd(_get_nc(), in_maps, core_ids=list(range(8)))
    return np.stack([r["out"] for r in res.results], axis=0)
